# version 38 (final)
# baseline (speedup 1.0000x reference)
import numpy as np

import concourse.bass as bass
import concourse.bass_utils as _bu
import concourse.mybir as mybir
from concourse.bass_utils import run_bass_kernel_spmd

if not getattr(_bu, "_ldw_opt_patched", False):
    _orig_run_command = _bu.run_command

    def _run_command_ldw(argv, **kwargs):
        if isinstance(argv, list):
            argv = ["--enable-ldw-opt=true" if a == "--enable-ldw-opt=false" else a
                    for a in argv]
        return _orig_run_command(argv, **kwargs)

    _bu.run_command = _run_command_ldw
    _bu._ldw_opt_patched = True

F32 = mybir.dt.float32
BF16 = mybir.dt.bfloat16
AF = mybir.ActivationFunctionType
OP = mybir.AluOpType

HD = 64


class Cfg:
    def __init__(self, D=1024, T=512, SEQ=4096, OWN=4096, INTER=4096, n_pairs=4):
        self.n_pairs = n_pairs
        self.D = D
        self.T = T
        self.SEQ = SEQ
        self.OWN = OWN
        self.INTER = INTER
        self.NCD = D // 128
        self.NH = D // HD
        self.NP = D // 128
        self.TC = T // 128
        self.NSLAB = SEQ // T
        self.NT = OWN // T
        self.INTC = INTER // 128
        self.NOG = INTER // 256
        self.KW = min(512, D)
        self.NKU = D // self.KW
        self.HIC = self.INTC // 2
        assert D % 128 == 0 and T % 128 == 0 and T <= 512 and INTER % 256 == 0
        assert SEQ % T == 0 and OWN % T == 0 and self.INTC % 2 == 0


PE, VE, ACT, GP = "pe", "ve", "act", "gp"


def pack_x(xh, c):
    return np.ascontiguousarray(
        xh.reshape(c.NCD, 128, c.NSLAB, c.T).transpose(1, 2, 0, 3)
        .reshape(128, c.NSLAB * c.NCD * c.T))


def unpack_out(o, c):
    return o.reshape(128, c.NT, c.NCD, c.T).transpose(2, 0, 1, 3).reshape(c.D, c.OWN)


def pack_w1(w1, c):
    return np.ascontiguousarray(
        w1.reshape(c.NCD, 128, c.NOG, 2 * 128).transpose(1, 2, 0, 3)
        .reshape(128, c.NOG * c.NCD * 256))


def pack_wres(w, c):
    return np.ascontiguousarray(
        w.reshape(c.NCD, 128, c.NKU, c.KW).transpose(1, 2, 0, 3)
        .reshape(128, c.NKU * c.NCD * c.KW))


def pack_w2(w2, c):
    return np.ascontiguousarray(
        w2.reshape(2, c.HIC, 128, c.NCD, 128).transpose(2, 3, 0, 1, 4)
        .reshape(128, c.NCD * 2 * c.HIC * 128))


def pack_bias(bs, c):
    cols = [np.asarray(bs[n], dtype=np.float32).reshape(-1, 128).T
            for n in ("bq", "bg", "bo", "b2", "b1")]
    return np.ascontiguousarray(np.concatenate(cols, axis=1))


class Emitter:
    ENGS = ("tensor", "vector", "scalar", "sync", "gpsimd")
    STATIC = ("g_wk0", "g_wk1", "g_wv0", "g_wv1", "g_wq", "g_wg", "g_wo",
              "g_bias", "s_bias", "s_cv0", "s_cv1", "g_kvl", "g_x0f")

    def __init__(self, nc):
        self.nc = nc
        self.ops = {e: [] for e in self.ENGS}
        self.sems = {}
        self.waited = {e: {} for e in self.ENGS}

    def sem(self, name):
        if name not in self.sems:
            self.sems[name] = [self.nc.alloc_semaphore(name), 0]
        return self.sems[name]

    def handle(self, name):
        return self.sem(name)[0]

    def reserve(self, name, k=1):
        s = self.sem(name)
        s[1] += k
        return s[1]

    def val(self, name):
        return self.sem(name)[1]

    def emit(self, eng, fn):
        self.ops[eng].append(fn)

    def wait(self, eng, name, v):
        if v is None or v <= 0:
            return
        if name in self.STATIC and self.waited[eng].get(name, -1) >= v:
            return
        s = self.sem(name)
        assert v <= s[1], f"wait {name}>={v} but only {s[1]} reserved"
        self.waited[eng][name] = v
        h = s[0]
        self.emit(eng, lambda e, h=h, v=v: e.wait_ge(h, v))

    def waits(self, eng, deps):
        seen = {}
        for d in deps:
            if d is None:
                continue
            n, v = d
            if v is None:
                continue
            seen[n] = max(seen.get(n, 0), v)
        for n, v in seen.items():
            self.wait(eng, n, v)

    def vop(self, fn, deps=()):
        self.waits("vector", deps)
        prev = self.val(VE)
        v = self.reserve(VE)
        self.wait("vector", VE, prev)
        h = self.handle(VE)
        self.emit("vector", lambda e, fn=fn, h=h: fn(e).then_inc(h, 1))
        return v

    def sop(self, fn, deps=()):
        self.waits("scalar", deps)
        prev = self.val(ACT)
        v = self.reserve(ACT)
        self.wait("scalar", ACT, prev)
        h = self.handle(ACT)
        self.emit("scalar", lambda e, fn=fn, h=h: fn(e).then_inc(h, 1))
        return v

    def run(self, block):
        for eng in self.ENGS:
            lst = self.ops[eng]

            def body(e, lst=lst):
                for fn in lst:
                    fn(e)

            getattr(block, eng)(body)


def build_nc(cfg):
    c = cfg
    nc = bass.Bass("TRN2", target_bir_lowering=False, debug=False)

    xf = nc.dram_tensor("xf", [128, c.NSLAB * c.NCD * c.T], F32, kind="ExternalInput").ap()
    wq_d = nc.dram_tensor("wq", [128, c.NCD * c.D], F32, kind="ExternalInput").ap()
    wk_d = nc.dram_tensor("wk", [128, c.NCD * c.D], F32, kind="ExternalInput").ap()
    wv_d = nc.dram_tensor("wv", [128, c.NCD * c.D], F32, kind="ExternalInput").ap()
    wg_d = nc.dram_tensor("wg", [128, c.NCD * c.D], F32, kind="ExternalInput").ap()
    wo_d = nc.dram_tensor("wo", [128, c.NCD * c.D], F32, kind="ExternalInput").ap()
    w1_d = nc.dram_tensor("w1", [128, c.NOG * 2 * c.NCD * 128], F32,
                          kind="ExternalInput").ap()
    w2_d = nc.dram_tensor("w2", [128, c.NCD * 2 * c.HIC * 128], F32,
                          kind="ExternalInput").ap()
    bias_d = {n: nc.dram_tensor(n, [1, sz], F32, kind="ExternalInput").ap()
              for n, sz in (("bq", c.D), ("bk", c.D), ("bv", c.D), ("bg", c.D),
                            ("bo", c.D), ("b1", c.INTER), ("b2", c.D))}
    bp_d = nc.dram_tensor("bp", [128, 4 * c.NCD + c.INTC], F32,
                          kind="ExternalInput").ap()
    out_d = nc.dram_tensor("out", [128, c.NT * c.NCD * c.T], F32,
                           kind="ExternalOutput").ap()
    w1s = nc.dram_tensor("w1s", [c.NOG * 128, 2 * c.NCD * 128], BF16, kind="Internal").ap()
    kvp_d = nc.dram_tensor("kvp", [128, c.NP * HD], F32, kind="Internal").ap()
    kvo_d = nc.dram_tensor("kvo", [128, c.NP * HD], F32, kind="Internal").ap()
    w2s = nc.dram_tensor("w2s", [c.NCD * 2 * 128, c.HIC * 128], BF16, kind="Internal").ap()

    A = nc.alloc_sbuf_tensor
    WRES = 3 * c.NCD * c.D
    W1SL = [WRES + i * 2 * c.NCD * 128 for i in range(3)]
    W2B = WRES + 3 * 2 * c.NCD * 128
    W2SL = [W2B + i * c.HIC * 128 for i in range(4)]
    wbuf = A("wbuf", [128, W2B + 4 * c.HIC * 128], BF16)
    x2b = A("x2b", [128, 2 * c.NCD * c.T], BF16)
    xb = A("xbuf", [128, 2 * c.NCD * c.T], BF16)
    qb = A("qb", [128, c.NCD * c.T], BF16)
    gb = A("gb", [128, c.NCD * c.T], BF16)
    rbA = A("rbA", [128, max(2 * c.NCD * c.T, 6 * c.D)], BF16)
    rbuf = A("rbuf", [128, c.NCD * c.T], F32)
    hbuf = A("hbuf", [128, max(c.INTC * c.T,
                               2 * max(2 * c.NCD * 128, c.HIC * 128))], BF16)
    kvsb = A("kvsb", [128, c.NP * HD], BF16)
    tmpA = A("tmpA", [128, 3 * 512], BF16)
    tmpB = A("tmpB", [128, 3 * 512], BF16)
    tmpC = A("tmpC", [128, 3 * 512], BF16)
    tmpN = A("tmpN", [128, 512], BF16)
    invB = A("invB", [128, 512], BF16)
    m2B = A("m2B", [128, 512], BF16)
    orow = A("orow", [1, 4 * 512], BF16)
    bqn_p = A("bqn_p", [128, c.NCD], F32)
    biasT = A("biasT", [128, 4 * c.NCD + c.INTC], F32)
    bq_p = biasT[:, 0:c.NCD]
    bg_p = biasT[:, c.NCD:2 * c.NCD]
    bo_p = biasT[:, 2 * c.NCD:3 * c.NCD]
    b2_p = biasT[:, 3 * c.NCD:4 * c.NCD]
    b1_p = biasT[:, 4 * c.NCD:4 * c.NCD + c.INTC]
    ones1r = A("ones1r", [1, 128], BF16)
    onescol = A("onescol", [128, 1], BF16)

    kbuf = rbA[:, 0:3 * c.D]
    vbuf = rbA[:, 3 * c.D:6 * c.D]
    bvB = x2b[:, 0:c.D]
    bkB = qb[:, 0:c.D]
    if c.NCD * c.T >= 3 * c.D:
        bkrow = qb[0:1, c.D:2 * c.D]
        bvrow = qb[0:1, 2 * c.D:3 * c.D]
    else:
        brow = A("brow", [1, 2 * c.D], BF16)
        bkrow = brow[0:1, 0:c.D]
        bvrow = brow[0:1, c.D:2 * c.D]

    def rb_tile(par):
        return rbA[:, par * c.NCD * c.T:(par + 1) * c.NCD * c.T]

    pb = [nc.alloc_psum_tensor(f"pb{i}", [128, 512], F32) for i in range(8)]
    KVB = 7

    em = Emitter(nc)

    def wv_view(lo, n_ic, width):
        return wbuf[:, lo:lo + n_ic * width].rearrange("p (i w) -> p i w", i=n_ic)

    def wsl(lo, ic, wstart, width):
        hf, rem = divmod(wstart, c.KW)
        base = lo + (hf * c.NCD + ic) * c.KW + rem
        return wbuf[:, base:base + width]

    def xb_ch(par, ic):
        base = (par * c.NCD + ic) * c.T
        return xb[:, base:base + c.T]

    def xb_lhs(par, ic, toff):
        base = (par * c.NCD + ic) * c.T + toff
        return xb[:, base:base + 128]

    def slot(buf, i, w=None):
        w = c.T if w is None else w
        return buf[:, i * w:(i + 1) * w]

    def rb_ch(par, ic):
        t = rb_tile(par)
        return t[:, ic * c.T:(ic + 1) * c.T]

    def x2t(par):
        return x2b[:, par * c.NCD * c.T:(par + 1) * c.NCD * c.T]

    def x2_ch(par, ic):
        t = x2t(par)
        return t[:, ic * c.T:(ic + 1) * c.T]

    bank_rot = [0]
    bank_free = [None] * 8
    bank_held = [False] * 8

    def acquire(pool):
        for _ in range(len(pool)):
            i = pool[bank_rot[0] % len(pool)]
            bank_rot[0] += 1
            if not bank_held[i]:
                return i
        raise RuntimeError("all psum banks held")

    def pe_group(deps, fns, mid=None):
        em.waits("tensor", deps)
        v = em.reserve(PE)
        h = em.handle(PE)
        for i, fn in enumerate(fns):
            if mid and i in mid:
                em.waits("tensor", mid[i])
            if i == len(fns) - 1:
                em.emit("tensor", lambda t, fn=fn, h=h: fn(t).then_inc(h, 1))
            else:
                em.emit("tensor", lambda t, fn=fn: fn(t))
        return v

    em.vop(lambda v: v.memset(ones1r[:], 1.0))
    vinit = em.vop(lambda v: v.memset(onescol[:], 1.0))

    em.reserve("s_bias", 16)
    em.emit("sync", lambda s: s.dma_start(out=biasT[:, :], in_=bp_d)
            .then_inc(em.handle("s_bias"), 16))
    bias_done = em.val("s_bias")
    em.reserve("g_bias", 16)
    em.emit("gpsimd", lambda g: g.dma_start(out=bkrow, in_=bias_d["bk"])
            .then_inc(em.handle("g_bias"), 16))
    em.reserve("g_bias", 16)
    em.emit("gpsimd", lambda g: g.dma_start(out=bvrow, in_=bias_d["bv"])
            .then_inc(em.handle("g_bias"), 16))
    bk_done = em.val("g_bias")

    em.vop(lambda v: v.tensor_scalar_mul(bqn_p[:, :], bq_p[:, :], -1.0),
           deps=[("s_bias", bias_done)])

    segs = [(j * 512, min(512, c.D - j * 512)) for j in range((c.D + 511) // 512)]
    fns = []
    for si, (off, w) in enumerate(segs):
        fns.append(lambda t, si=si, off=off, w=w:
                   t.matmul(pb[si][:, :w], ones1r[:, :], bvrow[0:1, off:off + w],
                            start=True, stop=True, skip_group_check=True))
    vpe = pe_group([("g_bias", bk_done), (VE, vinit)], fns)
    vbvB = None
    for si, (off, w) in enumerate(segs):
        vbvB = em.vop(lambda v, si=si, off=off, w=w:
                      v.tensor_copy(bvB[:, off:off + w], pb[si][:, :w]),
                      deps=[(PE, vpe)])
    for si in range(len(segs)):
        bank_free[si] = (VE, vbvB)

    WKO, WVO = c.NCD * c.D, 2 * c.NCD * c.D

    def ld_w_cast(dst_off, src, n_ic, width, semname):
        v = em.reserve(semname, 16)
        em.emit("gpsimd", lambda g, o=dst_off, so=src, ni=n_ic, w=width, sn=semname:
                g.dma_start(out=wbuf[:, o:o + ni * w], in_=so)
                .then_inc(em.handle(sn), 16))
        return v

    def ld_w_cast_half(dst_off, src_w, hf, semname):
        half = c.NCD * c.KW
        v = em.reserve(semname, 16)
        em.emit("gpsimd", lambda g, o=dst_off, so=src_w, hf=hf, sn=semname, half=half:
                g.dma_start(out=wbuf[:, o + hf * half:o + (hf + 1) * half],
                            in_=so[:, hf * half:(hf + 1) * half])
                .then_inc(em.handle(sn), 16))
        return v

    wa_ready = {}
    wa_ready[("k", 0)] = ld_w_cast_half(WKO, wk_d, 0, "g_wk0")

    conv_jobs = [("w1", g) for g in range(c.NOG)] + [("w2", j) for j in range(c.NCD * 2)]
    conv_war = [None, None]
    conv_store_final = [0, 0]
    conv_i = [0]

    def emit_conv_job(jidx):
        kind, j = conv_jobs[jidx]
        par = jidx % 2
        tmp_off = par * max(2 * c.NCD * 128, c.HIC * 128)
        if kind == "w1":
            n = 2 * c.NCD * 128
            src = w1_d[:, j * n:(j + 1) * n]
            dst = w1s[j * 128:(j + 1) * 128, :]
        else:
            n = c.HIC * 128
            src = w2_d[:, j * n:(j + 1) * n]
            dst = w2s[j * 128:(j + 1) * 128, :]
        tmp = hbuf[:, tmp_off:tmp_off + n]
        if conv_war[par] is not None:
            em.wait("gpsimd", f"s_cv{par}", conv_war[par])
        vg = em.reserve(f"g_cv{par}", 16)
        em.emit("gpsimd", lambda g, t=tmp, s=src, par=par:
                g.dma_start(out=t, in_=s).then_inc(em.handle(f"g_cv{par}"), 16))
        em.wait("sync", f"g_cv{par}", vg)
        vs = em.reserve(f"s_cv{par}", 16)
        em.emit("sync", lambda s, t=tmp, d=dst, par=par: s.dma_start(
            out=d, in_=t).then_inc(em.handle(f"s_cv{par}"), 16))
        conv_war[par] = vs
        conv_store_final[par] = vs

    def pump_conv(k):
        for _ in range(k):
            if conv_i[0] < len(conv_jobs):
                emit_conv_job(conv_i[0])
                conv_i[0] += 1

    slab_ready = [0, 0]
    slab_war = [None, None]

    def load_x_tile(global_t0, par):
        if slab_war[par] is not None:
            em.waits("gpsimd", [slab_war[par]])
        semn = f"g_x{par}"
        v = em.reserve(semn, 16)
        s_i = global_t0 // c.T
        em.emit("gpsimd", lambda g, par=par, s_i=s_i, sn=semn:
                g.dma_start(out=xb[:, par * c.NCD * c.T:(par + 1) * c.NCD * c.T],
                            in_=xf[:, s_i * c.NCD * c.T:(s_i + 1) * c.NCD * c.T])
                .then_inc(em.handle(sn), 16))
        slab_ready[par] = v

    kv_done_chunk = {}
    unit_par = [0]
    act_scratch_war = [None, None, None]

    def elu_unit(psum_i, dst_ap, pe_v, width, bias_pos=0.0, bias_neg=0.0,
                 bias_tile=None, bias_off=0, extra_dve_dep=None, use_dve=False):
        up = unit_par[0] % 3
        unit_par[0] += 1
        ta = tmpA[:, up * 512:up * 512 + width]
        tb = tmpB[:, up * 512:up * 512 + width]
        tc = tmpC[:, up * 512:up * 512 + width]
        if bias_tile is not None:
            td = tmpD[:, up * 512:up * 512 + width]
            ddeps = [(PE, pe_v), (VE, vbvB)]
            if act_scratch_war[up] is not None:
                ddeps.append((ACT, act_scratch_war[up][0]))
            pv = em.vop(lambda v, p=pb[psum_i], td=td, w=width, bo=bias_off:
                        v.tensor_add(td, p[:, :w], bias_tile[:, bo:bo + w]), deps=ddeps)
            bank_free[psum_i] = (VE, pv)
            em.sop(lambda e, td=td, ta=ta, w=width:
                   e.activation(ta, td, AF.Relu, bias=0.0, scale=1.0),
                   deps=[(VE, pv)])
            em.sop(lambda e, td=td, tb=tb, w=width:
                   e.activation(tb, td, AF.Relu, bias=0.0, scale=-1.0))
        elif use_dve:
            deps = [(PE, pe_v)]
            em.vop(lambda v, p=pb[psum_i], ta=ta, bp=bias_pos, w=width:
                   v.tensor_scalar(ta, p[:, :w], bp, 0.0, OP.add, OP.max), deps=deps)
            tdeps = []
            if act_scratch_war[up] is not None:
                tdeps.append((ACT, act_scratch_war[up][0]))
            v2 = em.vop(lambda v, p=pb[psum_i], tb=tb, bp=bias_pos, w=width:
                        v.tensor_scalar(tb, p[:, :w], bp, 0.0, OP.add, OP.min),
                        deps=tdeps)
            bank_free[psum_i] = (VE, v2)
            av = em.sop(lambda e, tb=tb, tc=tc:
                        e.activation(tc, tb, AF.Exp, bias=0.0, scale=1.0),
                        deps=[(VE, v2)])
            dd = [(ACT, av)]
            if extra_dve_dep is not None:
                dd.append(extra_dve_dep)
            vv = em.vop(lambda v, ta=ta, tc=tc, d=dst_ap: v.tensor_add(d, ta, tc),
                        deps=dd)
            act_scratch_war[up] = (av, vv)
            return vv
        else:
            deps = [(PE, pe_v)]
            if act_scratch_war[up] is not None:
                deps.append((VE, act_scratch_war[up][1]))
            em.sop(lambda e, p=pb[psum_i], ta=ta, bp=bias_pos, w=width:
                   e.activation(ta, p[:, :w], AF.Relu, bias=bp, scale=1.0), deps=deps)
            av2 = em.sop(lambda e, p=pb[psum_i], tb=tb, bn=bias_neg, w=width:
                         e.activation(tb, p[:, :w], AF.Relu, bias=bn, scale=-1.0))
            bank_free[psum_i] = (ACT, av2)
        av = em.sop(lambda e, tb=tb, tc=tc:
                    e.activation(tc, tb, AF.Exp, bias=0.0, scale=-1.0))
        dd = [(ACT, av)]
        if extra_dve_dep is not None:
            dd.append(extra_dve_dep)
        vv = em.vop(lambda v, ta=ta, tc=tc, d=dst_ap: v.tensor_add(d, ta, tc), deps=dd)
        act_scratch_war[up] = (av, vv)
        return vv

    poolA = [i for i in range(8) if i != KVB]
    NCHUNK = c.SEQ // 128
    pending_kv = []
    wq_ready = None

    first_chunk_ready = None
    for s in range(c.NSLAB):
        if s == 0 and c.T == 128:
            load_x_tile(0, 0)
            if c.NKU > 1:
                wa_ready[("k", 1)] = ld_w_cast_half(WKO, wk_d, 1, "g_wk1")
            wa_ready[("v", 0)] = ld_w_cast_half(WVO, wv_d, 0, "g_wv0")
            if c.NSLAB > 1:
                load_x_tile(c.T, 1)
            if c.NKU > 1:
                wa_ready[("v", 1)] = ld_w_cast_half(WVO, wv_d, 1, "g_wv1")
        elif s == 0:
            v0 = em.reserve("g_x0f", 16)
            em.emit("gpsimd", lambda g:
                    g.dma_start(out=xb[:, 0:c.NCD * c.T]
                                .rearrange("p (i t) -> p i t", i=c.NCD)[:, :, 0:128],
                                in_=xf[:, 0:c.NCD * c.T]
                                .rearrange("p (i t) -> p i t", i=c.NCD)[:, :, 0:128])
                    .then_inc(em.handle("g_x0f"), 16))
            first_chunk_ready = v0
            v1 = em.reserve("g_x0", 16)
            em.emit("gpsimd", lambda g:
                    g.dma_start(out=xb[:, 0:c.NCD * c.T]
                                .rearrange("p (i t) -> p i t", i=c.NCD)[:, :, 128:c.T],
                                in_=xf[:, 0:c.NCD * c.T]
                                .rearrange("p (i t) -> p i t", i=c.NCD)[:, :, 128:c.T])
                    .then_inc(em.handle("g_x0"), 16))
            if c.NKU > 1:
                wa_ready[("k", 1)] = ld_w_cast_half(WKO, wk_d, 1, "g_wk1")
            slab_ready[0] = v1
            wa_ready[("v", 0)] = ld_w_cast_half(WVO, wv_d, 0, "g_wv0")
            if c.NSLAB > 1:
                load_x_tile(c.T, 1)
            if c.NKU > 1:
                wa_ready[("v", 1)] = ld_w_cast_half(WVO, wv_d, 1, "g_wv1")
        par = s % 2
        pump_conv(4)
        if s == 2:
            wq_ready = ld_w_cast(0, wq_d, c.NCD, c.D, "g_wq")
        for tc_i in range(c.TC):
            ch = s * c.TC + tc_i
            cpar = ch % 3
            toff = tc_i * 128
            last_evac = None
            units = [(w, hf) for hf in range(c.NKU) for w in ("k", "v")]
            for what, hf in units:
                woff = WKO if what == "k" else WVO
                dbuf = kbuf if what == "k" else vbuf
                if True:
                    bi = acquire(poolA)
                    fns = []
                    for ic in range(c.NCD):
                        stop = (ic == c.NCD - 1) and what == "v"
                        fns.append(lambda t, bi=bi, par=par, ic=ic, toff=toff,
                                   woff=woff, hf=hf, stop=stop:
                                   t.matmul(pb[bi][:, :c.KW], xb_lhs(par, ic, toff),
                                            wsl(woff, ic, hf * c.KW, c.KW),
                                            start=(ic == 0), stop=stop, skip_group_check=True))
                    if what == "k":
                        fns.append(lambda t, bi=bi, hf=hf:
                                   t.matmul(pb[bi][:, :c.KW], ones1r[:, :],
                                            bkrow[0:1, hf * c.KW:(hf + 1) * c.KW],
                                            start=False, stop=True, skip_group_check=True))
                    if ch == 0 and first_chunk_ready:
                        xdep = ("g_x0f", first_chunk_ready)
                    else:
                        xdep = (f"g_x{par}", slab_ready[par])
                    deps0 = [bank_free[bi], xdep,
                             (f"g_w{what}{hf}", wa_ready[(what, hf)]),
                             ("g_bias", bk_done)]
                    if ch == 0:
                        deps0.append((VE, vinit))
                    pev = pe_group(deps0, fns)
                    bank_free[bi] = None
                    dst = dbuf[:, cpar * c.D + hf * c.KW: cpar * c.D + (hf + 1) * c.KW]
                    wdep = kv_done_chunk.get(ch - 3)
                    wdep = None if wdep is None else (PE, wdep)
                    if what == "k":
                        last_evac = elu_unit(bi, dst, pev, c.KW, extra_dve_dep=wdep,
                                             use_dve=(hf == 1))
                    else:
                        deps = [(PE, pev), (VE, vbvB)]
                        if wdep is not None:
                            deps.append(wdep)
                        vv = em.vop(lambda v, bi=bi, dst=dst, hf=hf:
                                    v.tensor_add(dst, pb[bi][:, :c.KW],
                                                 bvB[:, hf * c.KW:(hf + 1) * c.KW]),
                                    deps=deps)
                        bank_free[bi] = (VE, vv)
                        last_evac = vv
            fns = []
            for h in range(c.NH):
                j, r = divmod(h, 2)
                fns.append(lambda t, cpar=cpar, h=h, j=j, r=r, ch=ch:
                           t.matmul(pb[KVB][r * 64:(r + 1) * 64, j * HD:(j + 1) * HD],
                                    kbuf[:, cpar * c.D + h * HD:cpar * c.D + (h + 1) * HD],
                                    vbuf[:, cpar * c.D + h * HD:cpar * c.D + (h + 1) * HD],
                                    start=(ch == 0 and j == 0), stop=(ch == NCHUNK - 1),
                                    skip_group_check=True, tile_position=(0, r * 64)))
            pending_kv.append((fns, (VE, last_evac), ch))
            if len(pending_kv) > 2:
                p_fns, p_dep, p_ch = pending_kv.pop(0)
                kv_done_chunk[p_ch] = pe_group([p_dep], p_fns)
        slab_war[par] = (PE, em.val(PE))
        if s + 2 < c.NSLAB:
            load_x_tile((s + 2) * c.T, (s + 2) % 2)
    while pending_kv:
        p_fns, p_dep, p_ch = pending_kv.pop(0)
        kv_done_chunk[p_ch] = pe_group([p_dep], p_fns)

    if wq_ready is None:
        wq_ready = ld_w_cast(0, wq_d, c.NCD, c.D, "g_wq")
    pump_conv(len(conv_jobs))
    WQO, WGO, WOO = 0, c.NCD * c.D, 2 * c.NCD * c.D
    load_x_tile(0, 0)
    if c.NT > 1:
        load_x_tile(c.T, 1)
    loaded_tiles = {0, 1}
    phaseA_pe_end_early = kv_done_chunk[NCHUNK - 1]
    em.waits("gpsimd", [(PE, phaseA_pe_end_early)])
    wg_ready = ld_w_cast(WGO, wg_d, c.NCD, c.D, "g_wg")
    wo_ready = ld_w_cast(WOO, wo_d, c.NCD, c.D, "g_wo")
    kvstage = rbuf[:, 0:c.NP * HD]
    kv_ve = em.vop(lambda v: v.tensor_copy(kvstage, pb[KVB][:, :c.NP * HD]),
                   deps=[(PE, kv_done_chunk[NCHUNK - 1])])
    bank_free[KVB] = (VE, kv_ve)
    em.wait("sync", VE, kv_ve)
    em.reserve("s_kv", 16)
    em.emit("sync", lambda s: s.dma_start(out=kvp_d, in_=kvstage)
            .then_inc(em.handle("s_kv"), 16))
    em.wait("gpsimd", "s_kv", 16)
    rg = [[2 * i, 2 * i + 1] for i in range(c.n_pairs)]
    ccv = em.reserve(GP)
    em.emit("gpsimd", lambda g, rg=rg:
            g.collective_compute("AllReduce", OP.add, replica_groups=rg,
                                 ins=[kvp_d], outs=[kvo_d])
            .then_inc(em.handle(GP), 1))
    em.wait("gpsimd", GP, ccv)
    em.reserve("g_kvl", 16)
    em.emit("gpsimd", lambda g: g.dma_start(out=kvsb[:, :], in_=kvo_d)
            .then_inc(em.handle("g_kvl"), 16))
    kv_ev_dep = ("g_kvl", 16)
    kvstage_dep = ("s_kv", 16)
    phaseA_pe_end = kv_done_chunk[NCHUNK - 1]

    poolBC = list(range(8))

    EPS = 1e-5
    qb_war = [None]
    gb_pe_war = [None]
    gb_ve_war = [None]
    rb_war = [None, None]
    x2b_pe_war = [None, None]
    hbuf_pe_war = [None]
    orow_war = [None, None]
    last_store = [None]
    w1p_war = [None, None, None]
    w2p_war = [None, None, None, None]
    store_q = []

    if c.NCD * c.T >= 3 * 512:
        rows_t = rbuf
    else:
        rows_t = A("rows", [1, 3 * 512], F32)

    def ln_rows(tag, b_s, b_q, pe_stats):
        r0 = rows_t[0:1, 0:c.T]
        r1 = rows_t[0:1, 512:512 + c.T]
        r2 = rows_t[0:1, 1024:1024 + c.T]
        oinv = orow[0:1, tag * 1024:tag * 1024 + c.T]
        om2 = orow[0:1, tag * 1024 + 512:tag * 1024 + 512 + c.T]
        deps = [(PE, pe_stats), ("s_out", em.val("s_out"))]
        if orow_war[tag] is not None:
            deps.append((PE, orow_war[tag]))
        em.vop(lambda v, b=b_s: v.tensor_copy(r0, pb[b][0:1, :c.T]), deps=deps)
        evv = em.vop(lambda v, b=b_q: v.tensor_copy(r1, pb[b][0:1, :c.T]))
        bank_free[b_s] = (VE, evv)
        bank_free[b_q] = (VE, evv)
        bank_held[b_s] = bank_held[b_q] = False
        inv_d = 1.0 / c.D
        em.vop(lambda v: v.tensor_scalar_mul(r0, r0, inv_d))
        em.vop(lambda v: v.tensor_mul(r2, r0, r0))
        em.vop(lambda v: v.scalar_tensor_tensor(r1, r1, inv_d, r2, OP.mult, OP.subtract))
        vv = em.vop(lambda v: v.tensor_scalar_add(r1, r1, EPS))
        em.sop(lambda e: e.activation(r2, r1, AF.Ln, bias=0.0, scale=1.0),
               deps=[(VE, vv)])
        av = em.sop(lambda e, oinv=oinv: e.activation(oinv, r2, AF.Exp, bias=0.0, scale=-0.5))
        mv = em.vop(lambda v, om2=om2, oinv=oinv: v.tensor_mul(om2, r0, oinv),
                    deps=[(ACT, av)])
        return mv

    def ln_tail(tag, rows_done, rpar, writer, first_deps=()):
        oinv = orow[0:1, tag * 1024:tag * 1024 + c.T]
        om2 = orow[0:1, tag * 1024 + 512:tag * 1024 + 512 + c.T]
        b_i = acquire(poolBC)
        b_m = acquire(poolBC)
        bfns = [lambda t, b=b_i, oinv=oinv: t.matmul(pb[b][:, :c.T], ones1r[:, :], oinv,
                                                     start=True, stop=True,
                                                     skip_group_check=True),
                lambda t, b=b_m, om2=om2: t.matmul(pb[b][:, :c.T], ones1r[:, :], om2,
                                                   start=True, stop=True,
                                                   skip_group_check=True)]
        bv_pe = pe_group([bank_free[b_i], bank_free[b_m], (VE, rows_done)], bfns)
        orow_war[tag] = bv_pe
        em.vop(lambda v, b=b_i: v.tensor_copy(invB[:, :c.T], pb[b][:, :c.T]),
               deps=[(PE, bv_pe)])
        gv = em.vop(lambda v, b=b_m: v.tensor_copy(m2B[:, :c.T], pb[b][:, :c.T]))
        bank_free[b_i] = (VE, gv)
        bank_free[b_m] = (VE, gv)
        nv = None
        fd = list(first_deps)
        for ic in range(c.NCD):
            em.vop(lambda v, rpar=rpar, ic=ic:
                   v.tensor_mul(tmpN[:, :c.T], rb_ch(rpar, ic), invB[:, :c.T]),
                   deps=fd)
            fd = ()
            nv = writer(ic)
        return nv

    def proj_stage(t, par, rpar, kind, after_first=None, ga_deps=None):
        sq = gb if kind == "wo" else qb
        tag = 0 if kind == "wo" else 1
        b_s = acquire(poolBC)
        b_q = acquire(poolBC)
        stat_deps0 = [bank_free[b_s], bank_free[b_q]]
        bank_free[b_s] = bank_free[b_q] = None
        bank_held[b_s] = bank_held[b_q] = True
        pending_stats = []
        pe_last = None
        for oc in range(c.NCD):
            bi = acquire(poolBC)
            if kind == "wo":
                fns = []
                for ic in range(c.NCD):
                    fns.append(lambda t_, bi=bi, ic=ic, oc=oc:
                               t_.matmul(pb[bi][:, :c.T],
                                         wsl(WOO, ic, oc * 128, 128),
                                         slot(qb, ic), start=(ic == 0),
                                         stop=(ic == c.NCD - 1), skip_group_check=True))
                deps0 = [bank_free[bi], ("g_wo", wo_ready)]
                mid = None
                if ga_deps is not None and oc == 0:
                    deps0.append((VE, ga_deps[0]))
                    mid = {i: [(VE, ga_deps[i])] for i in range(1, c.NCD)}
                elif ga_deps is not None:
                    deps0.append((VE, ga_deps[-1]))
                pev = pe_group(deps0, fns, mid=mid)
            else:
                lvs = []
                slots = []
                for hh in range(2):
                    sl = (oc * 2 + hh) % 4
                    if not w2_gate_done[0]:
                        em.wait("gpsimd", "s_cv0", conv_store_final[0])
                        em.wait("gpsimd", "s_cv1", conv_store_final[1])
                        w2_gate_done[0] = True
                    em.wait("gpsimd", PE, w2p_war[sl])
                    semw = f"s_w2{sl}"
                    lv = em.reserve(semw, 16)
                    em.emit("gpsimd", lambda s, oc=oc, hh=hh, sl=sl, sn=semw:
                            s.dma_start(out=wbuf[:, W2SL[sl]:W2SL[sl] + c.HIC * 128],
                                        in_=w2s[(oc * 2 + hh) * 128:
                                                (oc * 2 + hh + 1) * 128, :])
                            .then_inc(em.handle(sn), 16))
                    lvs.append((semw, lv))
                    slots.append(sl)
                fns = []
                for ic in range(c.INTC):
                    sl = slots[ic // c.HIC]
                    fns.append(lambda t_, bi=bi, ic=ic, sl=sl:
                               t_.matmul(pb[bi][:, :c.T],
                                         wbuf[:, W2SL[sl] + (ic % c.HIC) * 128:
                                              W2SL[sl] + (ic % c.HIC) * 128 + 128],
                                         slot(hbuf, ic), start=(ic == 0),
                                         stop=(ic == c.INTC - 1), skip_group_check=True))
                pev = pe_group([bank_free[bi], lvs[0], hbuf_deps[0]], fns,
                               mid={c.HIC: [lvs[1]]})
                w2p_war[slots[0]] = pev
                w2p_war[slots[1]] = pev
            bank_free[bi] = None
            pe_last = pev
            if oc == 1 and after_first is not None:
                after_first()
            deps = [(PE, pev), ("s_bias", bias_done)]
            if oc == 0:
                if rb_war[rpar] is not None:
                    deps.append((PE, rb_war[rpar]))
                if kind == "wo" and gb_pe_war[0] is not None:
                    deps.append((PE, gb_pe_war[0]))
                if kind == "y" and qb_war[0] is not None:
                    deps.append((PE, qb_war[0]))
            if kind == "wo":
                em.vop(lambda v, bi=bi, oc=oc, par=par, rpar=rpar:
                       v.scalar_tensor_tensor(rb_ch(rpar, oc), pb[bi][:, :c.T],
                                              bo_p[:, oc:oc + 1], xb_ch(par, oc),
                                              OP.add, OP.add), deps=deps)
            else:
                em.vop(lambda v, bi=bi, oc=oc, rpar=rpar:
                       v.scalar_tensor_tensor(rb_ch(rpar, oc), pb[bi][:, :c.T],
                                              b2_p[:, oc:oc + 1], x2_ch(rpar, oc),
                                              OP.add, OP.add), deps=deps)
            vv_r = em.vop(lambda v, oc=oc, rpar=rpar, sq=sq:
                          v.tensor_mul(slot(sq, oc), rb_ch(rpar, oc), rb_ch(rpar, oc)))
            bank_free[bi] = (VE, vv_r)
            sfns = [lambda t_, b=b_s, oc=oc, rpar=rpar:
                    t_.matmul(pb[b][0:1, :c.T], onescol[:, :], rb_ch(rpar, oc),
                              start=(oc == 0), stop=(oc == c.NCD - 1),
                              skip_group_check=True),
                    lambda t_, b=b_q, oc=oc, sq=sq:
                    t_.matmul(pb[b][0:1, :c.T], onescol[:, :], slot(sq, oc),
                              start=(oc == 0), stop=(oc == c.NCD - 1),
                              skip_group_check=True)]
            sdeps = [(VE, vv_r)] + (stat_deps0 if oc == 0 else [])
            pending_stats.append((sdeps, sfns))
            if len(pending_stats) > 4:
                pe_group(*pending_stats.pop(0))
        while pending_stats:
            pe_stats = pe_group(*pending_stats.pop(0))
        rb_war[rpar] = pe_stats
        if kind == "wo":
            gb_pe_war[0] = pe_stats
            qb_war[0] = pe_last
        else:
            qb_war[0] = pe_stats
        rows_done = ln_rows(tag, b_s, b_q, pe_stats)
        return pe_last, pe_stats, rows_done

    hbuf_deps = [None]

    def s1(t):
        par = t % 2
        if t not in loaded_tiles:
            load_x_tile(t * c.T, par)
            loaded_tiles.add(t)
        semn = f"g_x{par}"
        xv = slab_ready[par]
        q_add_done = []
        g_done = []
        for what in ("q", "g"):
            woff = WQO if what == "q" else WGO
            for oc in range(c.NCD):
                bi = acquire(poolBC)
                fns = []
                for ic in range(c.NCD):
                    fns.append(lambda t_, bi=bi, par=par, ic=ic, oc=oc, woff=woff:
                               t_.matmul(pb[bi][:, :c.T],
                                         wsl(woff, ic, oc * 128, 128),
                                         xb_ch(par, ic), start=(ic == 0),
                                         stop=(ic == c.NCD - 1), skip_group_check=True))
                pev = pe_group([bank_free[bi], (semn, xv),
                                ("g_wq", wq_ready) if what == "q" else ("g_wg", wg_ready)],
                               fns)
                bank_free[bi] = None
                if what == "q":
                    dep = None if qb_war[0] is None else (PE, qb_war[0])
                    vvq = elu_unit(bi, slot(qb, oc), pev, c.T,
                                   bias_pos=bq_p[:, oc:oc + 1],
                                   bias_neg=bqn_p[:, oc:oc + 1],
                                   extra_dve_dep=dep)
                    q_add_done.append(vvq)
                else:
                    deps = [(PE, pev), ("s_bias", bias_done)]
                    if oc == 0:
                        if gb_ve_war[0] is not None:
                            deps.append((VE, gb_ve_war[0]))
                        if gb_pe_war[0] is not None:
                            deps.append((PE, gb_pe_war[0]))
                    av = em.sop(lambda e, bi=bi, oc=oc:
                                e.activation(slot(gb, oc), pb[bi][:, :c.T], AF.Sigmoid,
                                             bias=bg_p[:, oc:oc + 1], scale=1.0),
                                deps=deps)
                    bank_free[bi] = (ACT, av)
                    g_done.append(av)
        qb_war[0] = None
        ga_done = []
        for j in range(c.NP):
            bi = acquire(poolBC)
            fns = []
            for r in range(2):
                fns.append(lambda t_, bi=bi, j=j, r=r:
                           t_.matmul(pb[bi][r * 64:(r + 1) * 64, :c.T],
                                     kvsb[r * 64:(r + 1) * 64, j * HD:(j + 1) * HD],
                                     qb[r * 64:(r + 1) * 64, j * c.T:(j + 1) * c.T],
                                     start=True, stop=True, skip_group_check=True,
                                     tile_position=(r * 64, r * 64)))
            pev = pe_group([bank_free[bi], (VE, q_add_done[j]), kv_ev_dep], fns)
            bank_free[bi] = None
            vv2 = em.vop(lambda v, bi=bi, j=j:
                         v.tensor_mul(slot(qb, j), pb[bi][:, :c.T], slot(gb, j)),
                         deps=[(PE, pev), (ACT, g_done[j])])
            bank_free[bi] = (VE, vv2)
            ga_done.append(vv2)
        gb_ve_war[0] = ga_done[-1]
        em.waits("tensor", [(VE, ga_done[-1])])
        pe_last, pe_stats, rows_done = proj_stage(t, par, par, "wo")
        slab_war[par] = (VE, em.val(VE))
        return rows_done

    def h_stage(t):
        h_act_last = None
        if t == 0:
            em.wait("sync", "s_cv0", conv_store_final[0])
            em.wait("sync", "s_cv1", conv_store_final[1])
        pev = None
        for g_i in range(c.NOG):
            parw = g_i % 3
            em.wait("sync", PE, w1p_war[parw])
            semw = f"s_w1{parw}"
            lv = em.reserve(semw, 16)
            em.emit("sync", lambda s, g_i=g_i, parw=parw, sn=semw:
                    s.dma_start(out=wbuf[:, W1SL[parw]:W1SL[parw] + 2 * c.NCD * 128],
                                in_=w1s[g_i * 128:(g_i + 1) * 128, :])
                    .then_inc(em.handle(sn), 16))
            for o2 in range(2):
                oc = 2 * g_i + o2
                bi = acquire(poolBC)
                deps = [bank_free[bi], (semw, lv)]
                if oc == 0:
                    deps.append((VE, norm1_done[0]))
                fns = []
                for ic in range(c.NCD):
                    fns.append(lambda t_, bi=bi, ic=ic, o2=o2, parw=parw, tp=t % 2:
                               t_.matmul(pb[bi][:, :c.T],
                                         wbuf[:, W1SL[parw] + (ic * 2 + o2) * 128:
                                              W1SL[parw] + (ic * 2 + o2) * 128 + 128],
                                         x2_ch(tp, ic), start=(ic == 0),
                                         stop=(ic == c.NCD - 1), skip_group_check=True))
                pev = pe_group(deps, fns)
                bank_free[bi] = None
                sdeps = [(PE, pev), ("s_bias", bias_done)]
                if hbuf_pe_war[0] is not None:
                    sdeps.append((PE, hbuf_pe_war[0]))
                    hbuf_pe_war[0] = None
                av = em.sop(lambda e, bi=bi, oc=oc:
                            e.activation(slot(hbuf, oc), pb[bi][:, :c.T], AF.Relu,
                                         bias=b1_p[:, oc:oc + 1], scale=1.0),
                            deps=sdeps)
                bank_free[bi] = (ACT, av)
                h_act_last = av
            w1p_war[parw] = pev
        x2b_pe_war[t % 2] = pev
        hbuf_deps[0] = (ACT, h_act_last)

    norm1_done = [None]
    w2_gate_done = [False]
    kvstage_dep_box = [kvstage_dep]
    rows2_pending = None

    for t in range(c.NT):
        def write_x2(ic, tp=t % 2):
            return em.vop(lambda v, ic=ic, tp=tp:
                          v.tensor_sub(x2_ch(tp, ic), tmpN[:, :c.T], m2B[:, :c.T]))

        rows1_done = s1(t)

        def emit_ln1(t=t, rows1_done=rows1_done, write_x2=write_x2):
            fd = []
            if x2b_pe_war[t % 2] is not None:
                fd.append((PE, x2b_pe_war[t % 2]))
                x2b_pe_war[t % 2] = None
            norm1_done[0] = ln_tail(0, rows1_done, t % 2, write_x2, first_deps=fd)

        if t >= 1:
            y_last, y_stats, y_rows = proj_stage(t - 1, (t - 1) % 2, (t - 1) % 2, "y",
                                                 after_first=emit_ln1)
            hbuf_pe_war[0] = y_last
            rows2_pending = (t - 1, y_rows, y_last)
        else:
            emit_ln1()
        h_stage(t)
        if rows2_pending is not None and t >= 1:
            tp, r2_rows, _ = rows2_pending
            rows2_pending = None
            fd2 = []
            if kvstage_dep_box[0] is not None:
                fd2.append(kvstage_dep_box[0])
                kvstage_dep_box[0] = None
            if last_store[0] is not None:
                fd2.append(("s_out", last_store[0]))
                last_store[0] = None

            def write_x3(ic):
                return em.vop(lambda v, ic=ic:
                              v.tensor_sub(slot(rbuf, ic), tmpN[:, :c.T], m2B[:, :c.T]))
            nv2 = ln_tail(1, r2_rows, tp % 2, write_x3, first_deps=fd2)
            em.wait("sync", VE, nv2)
            sv = em.reserve("s_out", 16)
            t0s = tp * c.T
            em.emit("sync", lambda s, t0=t0s:
                    s.dma_start(out=out_d[:, t0 * c.NCD:(t0 + c.T) * c.NCD],
                                in_=rbuf[:, :c.NCD * c.T])
                    .then_inc(em.handle("s_out"), 16))
            last_store[0] = sv

    t = c.NT
    y_last, y_stats, y_rows = proj_stage(t - 1, (t - 1) % 2, (t - 1) % 2, "y")
    fd2 = []
    if kvstage_dep_box[0] is not None:
        fd2.append(kvstage_dep_box[0])
        kvstage_dep_box[0] = None
    if last_store[0] is not None:
        fd2.append(("s_out", last_store[0]))
        last_store[0] = None

    def write_x3f(ic):
        return em.vop(lambda v, ic=ic:
                      v.tensor_sub(slot(rbuf, ic), tmpN[:, :c.T], m2B[:, :c.T]))
    nv2 = ln_tail(1, y_rows, (t - 1) % 2, write_x3f, first_deps=fd2)
    em.wait("sync", VE, nv2)
    sv = em.reserve("s_out", 16)
    t0 = (t - 1) * c.T
    em.emit("sync", lambda s, t0=t0:
            s.dma_start(out=out_d[:, t0 * c.NCD:(t0 + c.T) * c.NCD],
                        in_=rbuf[:, :c.NCD * c.T])
            .then_inc(em.handle("s_out"), 16))
    last_store[0] = sv

    names = [n for n in em.sems if (n.startswith("s_") or n.startswith("g_")) and n != "s_out"]
    for n in names:
        if em.val(n) > 0:
            em.wait("sync", n, em.val(n))
    em.wait("sync", "s_out", em.val("s_out"))

    nc._dbg = dict(kvsb=kvsb, x2b=x2b, qb=qb, gb=gb, rbuf=rbuf, hbuf=hbuf,
                   rbA=rbA, bvB=bvB, invB=invB, m2B=m2B, orow=orow,
                   bq_p=bq_p, bkrow=bkrow, wbuf=wbuf, xb=xb)
    with nc.Block() as block:
        em.run(block)
    return nc


def kernel(L0_attn_x, L0_attn_Wq, L0_attn_bq, L0_attn_Wk, L0_attn_bk, L0_attn_Wv,
           L0_attn_bv, L0_attn_Wg, L0_attn_bg, L0_attn_Wo, L0_attn_bo,
           L0_ffn_W1, L0_ffn_b1, L0_ffn_W2, L0_ffn_b2):
    cfg = Cfg()
    x = np.asarray(L0_attn_x, dtype=np.float32)
    B, S, D = x.shape
    nc = build_nc(cfg)
    weights = {
        "wq": pack_wres(np.asarray(L0_attn_Wq, dtype=np.float32), cfg),
        "wk": pack_wres(np.asarray(L0_attn_Wk, dtype=np.float32), cfg),
        "wv": pack_wres(np.asarray(L0_attn_Wv, dtype=np.float32), cfg),
        "wg": pack_wres(np.asarray(L0_attn_Wg, dtype=np.float32), cfg),
        "wo": pack_wres(np.asarray(L0_attn_Wo, dtype=np.float32), cfg),
        "w1": pack_w1(np.asarray(L0_ffn_W1, dtype=np.float32), cfg),
        "w2": pack_w2(np.asarray(L0_ffn_W2, dtype=np.float32), cfg),
        "bq": np.ascontiguousarray(L0_attn_bq, dtype=np.float32).reshape(1, -1),
        "bk": np.ascontiguousarray(L0_attn_bk, dtype=np.float32).reshape(1, -1),
        "bv": np.ascontiguousarray(L0_attn_bv, dtype=np.float32).reshape(1, -1),
        "bg": np.ascontiguousarray(L0_attn_bg, dtype=np.float32).reshape(1, -1),
        "bo": np.ascontiguousarray(L0_attn_bo, dtype=np.float32).reshape(1, -1),
        "b1": np.ascontiguousarray(L0_ffn_b1, dtype=np.float32).reshape(1, -1),
        "b2": np.ascontiguousarray(L0_ffn_b2, dtype=np.float32).reshape(1, -1),
    }
    weights["bp"] = pack_bias(weights, cfg)
    half = cfg.OWN
    in_maps = []
    for core in range(8):
        b, hf = core // 2, core % 2
        m = dict(weights)
        m["xf"] = pack_x(np.ascontiguousarray(x[b, hf * half:(hf + 1) * half, :].T),
                         cfg)
        in_maps.append(m)
    res = run_bass_kernel_spmd(nc, in_maps, core_ids=list(range(8)))
    out = np.empty((B, S, D), dtype=np.float32)
    for core in range(8):
        b, hf = core // 2, core % 2
        out[b, hf * half:(hf + 1) * half, :] = unpack_out(res.results[core]["out"],
                                                          cfg).T
    return out


# version 39
# speedup vs baseline: 1.1776x; 1.1776x over previous
import numpy as np

import concourse.bass as bass
import concourse.bass_utils as _bu
import concourse.mybir as mybir
from concourse.bass_utils import run_bass_kernel_spmd

if not getattr(_bu, "_ldw_opt_patched", False):
    _orig_run_command = _bu.run_command

    def _run_command_ldw(argv, **kwargs):
        if isinstance(argv, list):
            argv = ["--enable-ldw-opt=true" if a == "--enable-ldw-opt=false" else a
                    for a in argv]
        return _orig_run_command(argv, **kwargs)

    _bu.run_command = _run_command_ldw
    _bu._ldw_opt_patched = True

F32 = mybir.dt.float32
BF16 = mybir.dt.bfloat16
AF = mybir.ActivationFunctionType
OP = mybir.AluOpType

HD = 64


class Cfg:
    def __init__(self, D=1024, T=512, SEQ=4096, OWN=4096, INTER=4096, n_pairs=4):
        self.n_pairs = n_pairs
        self.D = D
        self.T = T
        self.SEQ = SEQ
        self.OWN = OWN
        self.INTER = INTER
        self.NCD = D // 128
        self.NH = D // HD
        self.NP = D // 128
        self.TC = T // 128
        self.NSLAB = SEQ // T
        self.NT = OWN // T
        self.INTC = INTER // 128
        self.NOG = INTER // 256
        self.KW = min(512, D)
        self.NKU = D // self.KW
        self.HIC = self.INTC // 2
        assert D % 128 == 0 and T % 128 == 0 and T <= 512 and INTER % 256 == 0
        assert SEQ % T == 0 and OWN % T == 0 and self.INTC % 2 == 0


PE, VE, ACT, GP = "pe", "ve", "act", "gp"


def pack_x(xh, c):
    return np.ascontiguousarray(
        xh.reshape(c.NCD, 128, c.NSLAB, c.T).transpose(1, 2, 0, 3)
        .reshape(128, c.NSLAB * c.NCD * c.T))


def unpack_out(o, c):
    return o.reshape(128, c.NT, c.NCD, c.T).transpose(2, 0, 1, 3).reshape(c.D, c.OWN)


def pack_w1(w1, c):
    return np.ascontiguousarray(
        w1.reshape(c.NCD, 128, c.NOG, 2 * 128).transpose(1, 2, 0, 3)
        .reshape(128, c.NOG * c.NCD * 256))


def pack_wres(w, c):
    return np.ascontiguousarray(
        w.reshape(c.NCD, 128, c.NKU, c.KW).transpose(1, 2, 0, 3)
        .reshape(128, c.NKU * c.NCD * c.KW))


def pack_w2(w2, c):
    return np.ascontiguousarray(
        w2.reshape(2, c.HIC, 128, c.NCD, 128).transpose(2, 3, 0, 1, 4)
        .reshape(128, c.NCD * 2 * c.HIC * 128))


def pack_bias(bs, c):
    cols = [np.asarray(bs[n], dtype=np.float32).reshape(-1, 128).T
            for n in ("bq", "bg", "bo", "b2", "b1")]
    return np.ascontiguousarray(np.concatenate(cols, axis=1))


class Emitter:
    ENGS = ("tensor", "vector", "scalar", "sync", "gpsimd")
    STATIC = ("g_wk0", "g_wk1", "g_wv0", "g_wv1", "g_wq", "g_wg", "g_wo",
              "g_bias", "s_bias", "s_cv0", "s_cv1", "g_kvl", "g_x0f")

    def __init__(self, nc):
        self.nc = nc
        self.ops = {e: [] for e in self.ENGS}
        self.sems = {}
        self.waited = {e: {} for e in self.ENGS}

    def sem(self, name):
        if name not in self.sems:
            self.sems[name] = [self.nc.alloc_semaphore(name), 0]
        return self.sems[name]

    def handle(self, name):
        return self.sem(name)[0]

    def reserve(self, name, k=1):
        s = self.sem(name)
        s[1] += k
        return s[1]

    def val(self, name):
        return self.sem(name)[1]

    def emit(self, eng, fn):
        self.ops[eng].append(fn)

    def wait(self, eng, name, v):
        if v is None or v <= 0:
            return
        if name in self.STATIC and self.waited[eng].get(name, -1) >= v:
            return
        s = self.sem(name)
        assert v <= s[1], f"wait {name}>={v} but only {s[1]} reserved"
        self.waited[eng][name] = v
        h = s[0]
        self.emit(eng, lambda e, h=h, v=v: e.wait_ge(h, v))

    def waits(self, eng, deps):
        seen = {}
        for d in deps:
            if d is None:
                continue
            n, v = d
            if v is None:
                continue
            seen[n] = max(seen.get(n, 0), v)
        for n, v in seen.items():
            self.wait(eng, n, v)

    def vop(self, fn, deps=()):
        self.waits("vector", deps)
        prev = self.val(VE)
        v = self.reserve(VE)
        self.wait("vector", VE, prev)
        h = self.handle(VE)
        self.emit("vector", lambda e, fn=fn, h=h: fn(e).then_inc(h, 1))
        return v

    def sop(self, fn, deps=()):
        self.waits("scalar", deps)
        prev = self.val(ACT)
        v = self.reserve(ACT)
        self.wait("scalar", ACT, prev)
        h = self.handle(ACT)
        self.emit("scalar", lambda e, fn=fn, h=h: fn(e).then_inc(h, 1))
        return v

    def run(self, block):
        for eng in self.ENGS:
            lst = self.ops[eng]

            def body(e, lst=lst):
                for fn in lst:
                    fn(e)

            getattr(block, eng)(body)


def build_nc(cfg):
    c = cfg
    nc = bass.Bass("TRN2", target_bir_lowering=False, debug=False)

    xf = nc.dram_tensor("xf", [128, c.NSLAB * c.NCD * c.T], F32, kind="ExternalInput").ap()
    wq_d = nc.dram_tensor("wq", [128, c.NCD * c.D], F32, kind="ExternalInput").ap()
    wk_d = nc.dram_tensor("wk", [128, c.NCD * c.D], F32, kind="ExternalInput").ap()
    wv_d = nc.dram_tensor("wv", [128, c.NCD * c.D], F32, kind="ExternalInput").ap()
    wg_d = nc.dram_tensor("wg", [128, c.NCD * c.D], F32, kind="ExternalInput").ap()
    wo_d = nc.dram_tensor("wo", [128, c.NCD * c.D], F32, kind="ExternalInput").ap()
    w1_d = nc.dram_tensor("w1", [128, c.NOG * 2 * c.NCD * 128], F32,
                          kind="ExternalInput").ap()
    w2_d = nc.dram_tensor("w2", [128, c.NCD * 2 * c.HIC * 128], F32,
                          kind="ExternalInput").ap()
    bias_d = {n: nc.dram_tensor(n, [1, sz], F32, kind="ExternalInput").ap()
              for n, sz in (("bq", c.D), ("bk", c.D), ("bv", c.D), ("bg", c.D),
                            ("bo", c.D), ("b1", c.INTER), ("b2", c.D))}
    bp_d = nc.dram_tensor("bp", [128, 4 * c.NCD + c.INTC], F32,
                          kind="ExternalInput").ap()
    out_d = nc.dram_tensor("out", [128, c.NT * c.NCD * c.T], F32,
                           kind="ExternalOutput").ap()
    w1s = nc.dram_tensor("w1s", [c.NOG * 128, 2 * c.NCD * 128], BF16, kind="Internal").ap()
    kvp_d = nc.dram_tensor("kvp", [128, c.NP * HD], F32, kind="Internal").ap()
    kvo_d = nc.dram_tensor("kvo", [128, c.NP * HD], F32, kind="Internal").ap()
    w2s = nc.dram_tensor("w2s", [c.NCD * 2 * 128, c.HIC * 128], BF16, kind="Internal").ap()

    A = nc.alloc_sbuf_tensor
    WRES = 3 * c.NCD * c.D
    W1SL = [WRES + i * 2 * c.NCD * 128 for i in range(3)]
    W2B = WRES + 3 * 2 * c.NCD * 128
    W2SL = [W2B + i * c.HIC * 128 for i in range(4)]
    wbuf = A("wbuf", [128, W2B + 4 * c.HIC * 128], BF16)
    x2b = A("x2b", [128, 2 * c.NCD * c.T], BF16)
    xb = A("xbuf", [128, 2 * c.NCD * c.T], BF16)
    qb = A("qb", [128, c.NCD * c.T], BF16)
    gb = A("gb", [128, c.NCD * c.T], BF16)
    rbA = A("rbA", [128, max(2 * c.NCD * c.T, 6 * c.D)], BF16)
    rbuf = A("rbuf", [128, c.NCD * c.T], F32)
    hbuf = A("hbuf", [128, max(c.INTC * c.T,
                               2 * max(2 * c.NCD * 128, c.HIC * 128))], BF16)
    kvsb = A("kvsb", [128, c.NP * HD], BF16)
    tmpA = A("tmpA", [128, 3 * 512], BF16)
    tmpB = A("tmpB", [128, 3 * 512], BF16)
    tmpC = A("tmpC", [128, 3 * 512], BF16)
    tmpN = A("tmpN", [128, 512], BF16)
    invB = A("invB", [128, 512], BF16)
    m2B = A("m2B", [128, 512], BF16)
    orow = A("orow", [1, 4 * 512], BF16)
    bqn_p = A("bqn_p", [128, c.NCD], F32)
    biasT = A("biasT", [128, 4 * c.NCD + c.INTC], F32)
    bq_p = biasT[:, 0:c.NCD]
    bg_p = biasT[:, c.NCD:2 * c.NCD]
    bo_p = biasT[:, 2 * c.NCD:3 * c.NCD]
    b2_p = biasT[:, 3 * c.NCD:4 * c.NCD]
    b1_p = biasT[:, 4 * c.NCD:4 * c.NCD + c.INTC]
    ones1r = A("ones1r", [1, 128], BF16)
    onescol = A("onescol", [128, 1], BF16)

    kbuf = rbA[:, 0:3 * c.D]
    vbuf = rbA[:, 3 * c.D:6 * c.D]
    bvB = x2b[:, 0:c.D]
    bkB = qb[:, 0:c.D]
    if c.NCD * c.T >= 3 * c.D:
        bkrow = qb[0:1, c.D:2 * c.D]
        bvrow = qb[0:1, 2 * c.D:3 * c.D]
    else:
        brow = A("brow", [1, 2 * c.D], BF16)
        bkrow = brow[0:1, 0:c.D]
        bvrow = brow[0:1, c.D:2 * c.D]

    def rb_tile(par):
        return rbA[:, par * c.NCD * c.T:(par + 1) * c.NCD * c.T]

    pb = [nc.alloc_psum_tensor(f"pb{i}", [128, 512], F32) for i in range(8)]
    KVB = 7

    em = Emitter(nc)

    def wv_view(lo, n_ic, width):
        return wbuf[:, lo:lo + n_ic * width].rearrange("p (i w) -> p i w", i=n_ic)

    def wsl(lo, ic, wstart, width):
        hf, rem = divmod(wstart, c.KW)
        base = lo + (hf * c.NCD + ic) * c.KW + rem
        return wbuf[:, base:base + width]

    def xb_ch(par, ic):
        base = (par * c.NCD + ic) * c.T
        return xb[:, base:base + c.T]

    def xb_lhs(par, ic, toff):
        base = (par * c.NCD + ic) * c.T + toff
        return xb[:, base:base + 128]

    def slot(buf, i, w=None):
        w = c.T if w is None else w
        return buf[:, i * w:(i + 1) * w]

    def rb_ch(par, ic):
        t = rb_tile(par)
        return t[:, ic * c.T:(ic + 1) * c.T]

    def x2t(par):
        return x2b[:, par * c.NCD * c.T:(par + 1) * c.NCD * c.T]

    def x2_ch(par, ic):
        t = x2t(par)
        return t[:, ic * c.T:(ic + 1) * c.T]

    bank_rot = [0]
    bank_free = [None] * 8
    bank_held = [False] * 8

    def acquire(pool):
        for _ in range(len(pool)):
            i = pool[bank_rot[0] % len(pool)]
            bank_rot[0] += 1
            if not bank_held[i]:
                return i
        raise RuntimeError("all psum banks held")

    def pe_group(deps, fns, mid=None):
        em.waits("tensor", deps)
        v = em.reserve(PE)
        h = em.handle(PE)
        for i, fn in enumerate(fns):
            if mid and i in mid:
                em.waits("tensor", mid[i])
            if i == len(fns) - 1:
                em.emit("tensor", lambda t, fn=fn, h=h: fn(t).then_inc(h, 1))
            else:
                em.emit("tensor", lambda t, fn=fn: fn(t))
        return v

    em.vop(lambda v: v.memset(ones1r[:], 1.0))
    vinit = em.vop(lambda v: v.memset(onescol[:], 1.0))

    em.reserve("s_bias", 16)
    em.emit("sync", lambda s: s.dma_start(out=biasT[:, :], in_=bp_d)
            .then_inc(em.handle("s_bias"), 16))
    bias_done = em.val("s_bias")
    em.reserve("g_bias", 16)
    em.emit("gpsimd", lambda g: g.dma_start(out=bkrow, in_=bias_d["bk"])
            .then_inc(em.handle("g_bias"), 16))
    em.reserve("g_bias", 16)
    em.emit("gpsimd", lambda g: g.dma_start(out=bvrow, in_=bias_d["bv"])
            .then_inc(em.handle("g_bias"), 16))
    bk_done = em.val("g_bias")

    em.vop(lambda v: v.tensor_scalar_mul(bqn_p[:, :], bq_p[:, :], -1.0),
           deps=[("s_bias", bias_done)])

    segs = [(j * 512, min(512, c.D - j * 512)) for j in range((c.D + 511) // 512)]
    fns = []
    for si, (off, w) in enumerate(segs):
        fns.append(lambda t, si=si, off=off, w=w:
                   t.matmul(pb[si][:, :w], ones1r[:, :], bvrow[0:1, off:off + w],
                            start=True, stop=True, skip_group_check=True))
    vpe = pe_group([("g_bias", bk_done), (VE, vinit)], fns)
    vbvB = None
    for si, (off, w) in enumerate(segs):
        vbvB = em.vop(lambda v, si=si, off=off, w=w:
                      v.tensor_copy(bvB[:, off:off + w], pb[si][:, :w]),
                      deps=[(PE, vpe)])
    for si in range(len(segs)):
        bank_free[si] = (VE, vbvB)

    WKO, WVO = c.NCD * c.D, 2 * c.NCD * c.D

    def ld_w_cast(dst_off, src, n_ic, width, semname):
        v = em.reserve(semname, 16)
        em.emit("gpsimd", lambda g, o=dst_off, so=src, ni=n_ic, w=width, sn=semname:
                g.dma_start(out=wbuf[:, o:o + ni * w], in_=so)
                .then_inc(em.handle(sn), 16))
        return v

    def ld_w_cast_half(dst_off, src_w, hf, semname):
        half = c.NCD * c.KW
        v = em.reserve(semname, 16)
        em.emit("gpsimd", lambda g, o=dst_off, so=src_w, hf=hf, sn=semname, half=half:
                g.dma_start(out=wbuf[:, o + hf * half:o + (hf + 1) * half],
                            in_=so[:, hf * half:(hf + 1) * half])
                .then_inc(em.handle(sn), 16))
        return v

    wa_ready = {}
    wa_ready[("k", 0)] = ld_w_cast_half(WKO, wk_d, 0, "g_wk0")

    conv_jobs = [("w1", g) for g in range(c.NOG)] + [("w2", j) for j in range(c.NCD * 2)]
    conv_war = [None, None]
    conv_store_final = [0, 0]
    conv_i = [0]

    def emit_conv_job(jidx):
        kind, j = conv_jobs[jidx]
        par = jidx % 2
        tmp_off = par * max(2 * c.NCD * 128, c.HIC * 128)
        if kind == "w1":
            n = 2 * c.NCD * 128
            src = w1_d[:, j * n:(j + 1) * n]
            dst = w1s[j * 128:(j + 1) * 128, :]
        else:
            n = c.HIC * 128
            src = w2_d[:, j * n:(j + 1) * n]
            dst = w2s[j * 128:(j + 1) * 128, :]
        tmp = hbuf[:, tmp_off:tmp_off + n]
        if conv_war[par] is not None:
            em.wait("gpsimd", f"s_cv{par}", conv_war[par])
        vg = em.reserve(f"g_cv{par}", 16)
        em.emit("gpsimd", lambda g, t=tmp, s=src, par=par:
                g.dma_start(out=t, in_=s).then_inc(em.handle(f"g_cv{par}"), 16))
        em.wait("sync", f"g_cv{par}", vg)
        vs = em.reserve(f"s_cv{par}", 16)
        em.emit("sync", lambda s, t=tmp, d=dst, par=par: s.dma_start(
            out=d, in_=t).then_inc(em.handle(f"s_cv{par}"), 16))
        conv_war[par] = vs
        conv_store_final[par] = vs

    def pump_conv(k):
        for _ in range(k):
            if conv_i[0] < len(conv_jobs):
                emit_conv_job(conv_i[0])
                conv_i[0] += 1

    slab_ready = [0, 0]
    slab_war = [None, None]

    def load_x_tile(global_t0, par):
        if slab_war[par] is not None:
            em.waits("gpsimd", [slab_war[par]])
        semn = f"g_x{par}"
        v = em.reserve(semn, 16)
        s_i = global_t0 // c.T
        em.emit("gpsimd", lambda g, par=par, s_i=s_i, sn=semn:
                g.dma_start(out=xb[:, par * c.NCD * c.T:(par + 1) * c.NCD * c.T],
                            in_=xf[:, s_i * c.NCD * c.T:(s_i + 1) * c.NCD * c.T])
                .then_inc(em.handle(sn), 16))
        slab_ready[par] = v

    kv_done_chunk = {}
    unit_par = [0]
    act_scratch_war = [None, None, None]

    def elu_unit(psum_i, dst_ap, pe_v, width, bias_pos=0.0, bias_neg=0.0,
                 bias_tile=None, bias_off=0, extra_dve_dep=None, use_dve=False):
        up = unit_par[0] % 3
        unit_par[0] += 1
        ta = tmpA[:, up * 512:up * 512 + width]
        tb = tmpB[:, up * 512:up * 512 + width]
        tc = tmpC[:, up * 512:up * 512 + width]
        if bias_tile is not None:
            td = tmpD[:, up * 512:up * 512 + width]
            ddeps = [(PE, pe_v), (VE, vbvB)]
            if act_scratch_war[up] is not None:
                ddeps.append((ACT, act_scratch_war[up][0]))
            pv = em.vop(lambda v, p=pb[psum_i], td=td, w=width, bo=bias_off:
                        v.tensor_add(td, p[:, :w], bias_tile[:, bo:bo + w]), deps=ddeps)
            bank_free[psum_i] = (VE, pv)
            em.sop(lambda e, td=td, ta=ta, w=width:
                   e.activation(ta, td, AF.Relu, bias=0.0, scale=1.0),
                   deps=[(VE, pv)])
            em.sop(lambda e, td=td, tb=tb, w=width:
                   e.activation(tb, td, AF.Relu, bias=0.0, scale=-1.0))
        elif use_dve:
            deps = [(PE, pe_v)]
            em.vop(lambda v, p=pb[psum_i], ta=ta, bp=bias_pos, w=width:
                   v.tensor_scalar(ta, p[:, :w], bp, 0.0, OP.add, OP.max), deps=deps)
            tdeps = []
            if act_scratch_war[up] is not None:
                tdeps.append((ACT, act_scratch_war[up][0]))
            v2 = em.vop(lambda v, p=pb[psum_i], tb=tb, bp=bias_pos, w=width:
                        v.tensor_scalar(tb, p[:, :w], bp, 0.0, OP.add, OP.min),
                        deps=tdeps)
            bank_free[psum_i] = (VE, v2)
            av = em.sop(lambda e, tb=tb, tc=tc:
                        e.activation(tc, tb, AF.Exp, bias=0.0, scale=1.0),
                        deps=[(VE, v2)])
            dd = [(ACT, av)]
            if extra_dve_dep is not None:
                dd.append(extra_dve_dep)
            vv = em.vop(lambda v, ta=ta, tc=tc, d=dst_ap: v.tensor_add(d, ta, tc),
                        deps=dd)
            act_scratch_war[up] = (av, vv)
            return vv
        else:
            deps = [(PE, pe_v)]
            if act_scratch_war[up] is not None:
                deps.append((VE, act_scratch_war[up][1]))
            em.sop(lambda e, p=pb[psum_i], ta=ta, bp=bias_pos, w=width:
                   e.activation(ta, p[:, :w], AF.Relu, bias=bp, scale=1.0), deps=deps)
            av2 = em.sop(lambda e, p=pb[psum_i], tb=tb, bn=bias_neg, w=width:
                         e.activation(tb, p[:, :w], AF.Relu, bias=bn, scale=-1.0))
            bank_free[psum_i] = (ACT, av2)
        av = em.sop(lambda e, tb=tb, tc=tc:
                    e.activation(tc, tb, AF.Exp, bias=0.0, scale=-1.0))
        dd = [(ACT, av)]
        if extra_dve_dep is not None:
            dd.append(extra_dve_dep)
        vv = em.vop(lambda v, ta=ta, tc=tc, d=dst_ap: v.tensor_add(d, ta, tc), deps=dd)
        act_scratch_war[up] = (av, vv)
        return vv

    poolA = [i for i in range(8) if i != KVB]
    NCHUNK = c.SEQ // 128
    pending_kv = []
    wq_ready = None

    first_chunk_ready = None
    for s in range(c.NSLAB):
        if s == 0 and c.T == 128:
            load_x_tile(0, 0)
            if c.NKU > 1:
                wa_ready[("k", 1)] = ld_w_cast_half(WKO, wk_d, 1, "g_wk1")
            wa_ready[("v", 0)] = ld_w_cast_half(WVO, wv_d, 0, "g_wv0")
            if c.NSLAB > 1:
                load_x_tile(c.T, 1)
            if c.NKU > 1:
                wa_ready[("v", 1)] = ld_w_cast_half(WVO, wv_d, 1, "g_wv1")
        elif s == 0:
            v0 = em.reserve("g_x0f", 16)
            em.emit("gpsimd", lambda g:
                    g.dma_start(out=xb[:, 0:c.NCD * c.T]
                                .rearrange("p (i t) -> p i t", i=c.NCD)[:, :, 0:128],
                                in_=xf[:, 0:c.NCD * c.T]
                                .rearrange("p (i t) -> p i t", i=c.NCD)[:, :, 0:128])
                    .then_inc(em.handle("g_x0f"), 16))
            first_chunk_ready = v0
            v1 = em.reserve("g_x0", 16)
            em.emit("gpsimd", lambda g:
                    g.dma_start(out=xb[:, 0:c.NCD * c.T]
                                .rearrange("p (i t) -> p i t", i=c.NCD)[:, :, 128:c.T],
                                in_=xf[:, 0:c.NCD * c.T]
                                .rearrange("p (i t) -> p i t", i=c.NCD)[:, :, 128:c.T])
                    .then_inc(em.handle("g_x0"), 16))
            if c.NKU > 1:
                wa_ready[("k", 1)] = ld_w_cast_half(WKO, wk_d, 1, "g_wk1")
            slab_ready[0] = v1
            wa_ready[("v", 0)] = ld_w_cast_half(WVO, wv_d, 0, "g_wv0")
            if c.NSLAB > 1:
                load_x_tile(c.T, 1)
            if c.NKU > 1:
                wa_ready[("v", 1)] = ld_w_cast_half(WVO, wv_d, 1, "g_wv1")
        par = s % 2
        pump_conv(4)
        if s == 2:
            wq_ready = ld_w_cast(0, wq_d, c.NCD, c.D, "g_wq")
        for tc_i in range(c.TC):
            ch = s * c.TC + tc_i
            cpar = ch % 3
            toff = tc_i * 128
            last_evac = None
            units = [(w, hf) for hf in range(c.NKU) for w in ("k", "v")]
            for what, hf in units:
                woff = WKO if what == "k" else WVO
                dbuf = kbuf if what == "k" else vbuf
                if True:
                    bi = acquire(poolA)
                    fns = []
                    for ic in range(c.NCD):
                        stop = (ic == c.NCD - 1) and what == "v"
                        fns.append(lambda t, bi=bi, par=par, ic=ic, toff=toff,
                                   woff=woff, hf=hf, stop=stop:
                                   t.matmul(pb[bi][:, :c.KW], xb_lhs(par, ic, toff),
                                            wsl(woff, ic, hf * c.KW, c.KW),
                                            start=(ic == 0), stop=stop, skip_group_check=True))
                    if what == "k":
                        fns.append(lambda t, bi=bi, hf=hf:
                                   t.matmul(pb[bi][:, :c.KW], ones1r[:, :],
                                            bkrow[0:1, hf * c.KW:(hf + 1) * c.KW],
                                            start=False, stop=True, skip_group_check=True))
                    if ch == 0 and first_chunk_ready:
                        xdep = ("g_x0f", first_chunk_ready)
                    else:
                        xdep = (f"g_x{par}", slab_ready[par])
                    deps0 = [bank_free[bi], xdep,
                             (f"g_w{what}{hf}", wa_ready[(what, hf)]),
                             ("g_bias", bk_done)]
                    if ch == 0:
                        deps0.append((VE, vinit))
                    pev = pe_group(deps0, fns)
                    bank_free[bi] = None
                    dst = dbuf[:, cpar * c.D + hf * c.KW: cpar * c.D + (hf + 1) * c.KW]
                    wdep = kv_done_chunk.get(ch - 3)
                    wdep = None if wdep is None else (PE, wdep)
                    if what == "k":
                        last_evac = elu_unit(bi, dst, pev, c.KW, extra_dve_dep=wdep,
                                             use_dve=(hf == 1))
                    else:
                        deps = [(PE, pev), (VE, vbvB)]
                        if wdep is not None:
                            deps.append(wdep)
                        vv = em.vop(lambda v, bi=bi, dst=dst, hf=hf:
                                    v.tensor_add(dst, pb[bi][:, :c.KW],
                                                 bvB[:, hf * c.KW:(hf + 1) * c.KW]),
                                    deps=deps)
                        bank_free[bi] = (VE, vv)
                        last_evac = vv
            fns = []
            for h in range(c.NH):
                j, r = divmod(h, 2)
                fns.append(lambda t, cpar=cpar, h=h, j=j, r=r, ch=ch:
                           t.matmul(pb[KVB][r * 64:(r + 1) * 64, j * HD:(j + 1) * HD],
                                    kbuf[:, cpar * c.D + h * HD:cpar * c.D + (h + 1) * HD],
                                    vbuf[:, cpar * c.D + h * HD:cpar * c.D + (h + 1) * HD],
                                    start=(ch == 0 and j == 0), stop=(ch == NCHUNK - 1),
                                    skip_group_check=True, tile_position=(0, r * 64)))
            pending_kv.append((fns, (VE, last_evac), ch))
            if len(pending_kv) > 2:
                p_fns, p_dep, p_ch = pending_kv.pop(0)
                kv_done_chunk[p_ch] = pe_group([p_dep], p_fns)
        slab_war[par] = (PE, em.val(PE))
        if s + 2 < c.NSLAB:
            load_x_tile((s + 2) * c.T, (s + 2) % 2)
    while pending_kv:
        p_fns, p_dep, p_ch = pending_kv.pop(0)
        kv_done_chunk[p_ch] = pe_group([p_dep], p_fns)

    if wq_ready is None:
        wq_ready = ld_w_cast(0, wq_d, c.NCD, c.D, "g_wq")
    pump_conv(len(conv_jobs))
    WQO, WGO, WOO = 0, c.NCD * c.D, 2 * c.NCD * c.D
    load_x_tile(0, 0)
    if c.NT > 1:
        load_x_tile(c.T, 1)
    loaded_tiles = {0, 1}
    phaseA_pe_end_early = kv_done_chunk[NCHUNK - 1]
    em.waits("gpsimd", [(PE, phaseA_pe_end_early)])
    wg_ready = ld_w_cast(WGO, wg_d, c.NCD, c.D, "g_wg")
    wo_ready = ld_w_cast(WOO, wo_d, c.NCD, c.D, "g_wo")
    kvstage = rbuf[:, 0:c.NP * HD]
    kv_ve = em.vop(lambda v: v.tensor_copy(kvstage, pb[KVB][:, :c.NP * HD]),
                   deps=[(PE, kv_done_chunk[NCHUNK - 1])])
    bank_free[KVB] = (VE, kv_ve)
    em.wait("sync", VE, kv_ve)
    em.reserve("s_kv", 16)
    em.emit("sync", lambda s: s.dma_start(out=kvp_d, in_=kvstage)
            .then_inc(em.handle("s_kv"), 16))
    em.wait("gpsimd", "s_kv", 16)
    rg = [[2 * i, 2 * i + 1] for i in range(c.n_pairs)]
    ccv = em.reserve(GP)
    em.emit("gpsimd", lambda g, rg=rg:
            g.collective_compute("AllReduce", OP.add, replica_groups=rg,
                                 ins=[kvp_d], outs=[kvo_d])
            .then_inc(em.handle(GP), 1))
    em.wait("gpsimd", GP, ccv)
    em.reserve("g_kvl", 16)
    em.emit("gpsimd", lambda g: g.dma_start(out=kvsb[:, :], in_=kvo_d)
            .then_inc(em.handle("g_kvl"), 16))
    kv_ev_dep = ("g_kvl", 16)
    kvstage_dep = ("s_kv", 16)
    phaseA_pe_end = kv_done_chunk[NCHUNK - 1]

    poolBC = list(range(8))

    EPS = 1e-5
    qb_war = [None]
    gb_pe_war = [None]
    gb_ve_war = [None]
    rb_war = [None, None]
    x2b_pe_war = [None, None]
    hbuf_pe_war = [None]
    orow_war = [None, None]
    last_store = [None]
    w1p_war = [None, None, None]
    w2p_war = [None, None, None, None]
    store_q = []

    if c.NCD * c.T >= 3 * 512:
        rows_t = rbuf
    else:
        rows_t = A("rows", [1, 3 * 512], F32)

    def ln_rows(tag, b_s, b_q, pe_stats):
        r0 = rows_t[0:1, 0:c.T]
        r1 = rows_t[0:1, 512:512 + c.T]
        r2 = rows_t[0:1, 1024:1024 + c.T]
        oinv = orow[0:1, tag * 1024:tag * 1024 + c.T]
        om2 = orow[0:1, tag * 1024 + 512:tag * 1024 + 512 + c.T]
        deps = [(PE, pe_stats), ("s_out", em.val("s_out"))]
        if orow_war[tag] is not None:
            deps.append((PE, orow_war[tag]))
        em.vop(lambda v, b=b_s: v.tensor_copy(r0, pb[b][0:1, :c.T]), deps=deps)
        evv = em.vop(lambda v, b=b_q: v.tensor_copy(r1, pb[b][0:1, :c.T]))
        bank_free[b_s] = (VE, evv)
        bank_free[b_q] = (VE, evv)
        bank_held[b_s] = bank_held[b_q] = False
        inv_d = 1.0 / c.D
        em.vop(lambda v: v.tensor_scalar_mul(r0, r0, inv_d))
        em.vop(lambda v: v.tensor_mul(r2, r0, r0))
        em.vop(lambda v: v.scalar_tensor_tensor(r1, r1, inv_d, r2, OP.mult, OP.subtract))
        vv = em.vop(lambda v: v.tensor_scalar_add(r1, r1, EPS))
        em.sop(lambda e: e.activation(r2, r1, AF.Ln, bias=0.0, scale=1.0),
               deps=[(VE, vv)])
        av = em.sop(lambda e, oinv=oinv: e.activation(oinv, r2, AF.Exp, bias=0.0, scale=-0.5))
        mv = em.vop(lambda v, om2=om2, oinv=oinv: v.tensor_mul(om2, r0, oinv),
                    deps=[(ACT, av)])
        return mv

    def ln_tail(tag, rows_done, rpar, writer, first_deps=()):
        oinv = orow[0:1, tag * 1024:tag * 1024 + c.T]
        om2 = orow[0:1, tag * 1024 + 512:tag * 1024 + 512 + c.T]
        b_i = acquire(poolBC)
        b_m = acquire(poolBC)
        bfns = [lambda t, b=b_i, oinv=oinv: t.matmul(pb[b][:, :c.T], ones1r[:, :], oinv,
                                                     start=True, stop=True,
                                                     skip_group_check=True),
                lambda t, b=b_m, om2=om2: t.matmul(pb[b][:, :c.T], ones1r[:, :], om2,
                                                   start=True, stop=True,
                                                   skip_group_check=True)]
        bv_pe = pe_group([bank_free[b_i], bank_free[b_m], (VE, rows_done)], bfns)
        orow_war[tag] = bv_pe
        em.vop(lambda v, b=b_i: v.tensor_copy(invB[:, :c.T], pb[b][:, :c.T]),
               deps=[(PE, bv_pe)])
        gv = em.vop(lambda v, b=b_m: v.tensor_copy(m2B[:, :c.T], pb[b][:, :c.T]))
        bank_free[b_i] = (VE, gv)
        bank_free[b_m] = (VE, gv)
        nv = None
        fd = list(first_deps)
        for ic in range(c.NCD):
            em.vop(lambda v, rpar=rpar, ic=ic:
                   v.tensor_mul(tmpN[:, :c.T], rb_ch(rpar, ic), invB[:, :c.T]),
                   deps=fd)
            fd = ()
            nv = writer(ic)
        return nv

    def proj_stage(t, par, rpar, kind, after_first=None, ga_deps=None):
        sq = gb if kind == "wo" else qb
        tag = 0 if kind == "wo" else 1
        b_s = acquire(poolBC)
        b_q = acquire(poolBC)
        stat_deps0 = [bank_free[b_s], bank_free[b_q]]
        bank_free[b_s] = bank_free[b_q] = None
        bank_held[b_s] = bank_held[b_q] = True
        pending_stats = []
        pe_last = None
        for oc in range(c.NCD):
            bi = acquire(poolBC)
            if kind == "wo":
                fns = []
                for ic in range(c.NCD):
                    fns.append(lambda t_, bi=bi, ic=ic, oc=oc:
                               t_.matmul(pb[bi][:, :c.T],
                                         wsl(WOO, ic, oc * 128, 128),
                                         slot(qb, ic), start=(ic == 0),
                                         stop=(ic == c.NCD - 1), skip_group_check=True))
                deps0 = [bank_free[bi], ("g_wo", wo_ready)]
                mid = None
                if ga_deps is not None and oc == 0:
                    deps0.append((VE, ga_deps[0]))
                    mid = {i: [(VE, ga_deps[i])] for i in range(1, c.NCD)}
                elif ga_deps is not None:
                    deps0.append((VE, ga_deps[-1]))
                pev = pe_group(deps0, fns, mid=mid)
            else:
                lvs = []
                slots = []
                for hh in range(2):
                    sl = (oc * 2 + hh) % 4
                    if not w2_gate_done[0]:
                        em.wait("gpsimd", "s_cv0", conv_store_final[0])
                        em.wait("gpsimd", "s_cv1", conv_store_final[1])
                        w2_gate_done[0] = True
                    em.wait("gpsimd", PE, w2p_war[sl])
                    semw = f"s_w2{sl}"
                    lv = em.reserve(semw, 16)
                    em.emit("gpsimd", lambda s, oc=oc, hh=hh, sl=sl, sn=semw:
                            s.dma_start(out=wbuf[:, W2SL[sl]:W2SL[sl] + c.HIC * 128],
                                        in_=w2s[(oc * 2 + hh) * 128:
                                                (oc * 2 + hh + 1) * 128, :])
                            .then_inc(em.handle(sn), 16))
                    lvs.append((semw, lv))
                    slots.append(sl)
                fns = []
                for ic in range(c.INTC):
                    sl = slots[ic // c.HIC]
                    fns.append(lambda t_, bi=bi, ic=ic, sl=sl:
                               t_.matmul(pb[bi][:, :c.T],
                                         wbuf[:, W2SL[sl] + (ic % c.HIC) * 128:
                                              W2SL[sl] + (ic % c.HIC) * 128 + 128],
                                         slot(hbuf, ic), start=(ic == 0),
                                         stop=(ic == c.INTC - 1), skip_group_check=True))
                pev = pe_group([bank_free[bi], lvs[0], hbuf_deps[0]], fns,
                               mid={c.HIC: [lvs[1]]})
                w2p_war[slots[0]] = pev
                w2p_war[slots[1]] = pev
            bank_free[bi] = None
            pe_last = pev
            if oc == 1 and after_first is not None:
                after_first()
            deps = [(PE, pev), ("s_bias", bias_done)]
            if oc == 0:
                if rb_war[rpar] is not None:
                    deps.append((PE, rb_war[rpar]))
                if kind == "wo" and gb_pe_war[0] is not None:
                    deps.append((PE, gb_pe_war[0]))
                if kind == "y" and qb_war[0] is not None:
                    deps.append((PE, qb_war[0]))
            if kind == "wo":
                em.vop(lambda v, bi=bi, oc=oc, par=par, rpar=rpar:
                       v.scalar_tensor_tensor(rb_ch(rpar, oc), pb[bi][:, :c.T],
                                              bo_p[:, oc:oc + 1], xb_ch(par, oc),
                                              OP.add, OP.add), deps=deps)
            else:
                em.vop(lambda v, bi=bi, oc=oc, rpar=rpar:
                       v.scalar_tensor_tensor(rb_ch(rpar, oc), pb[bi][:, :c.T],
                                              b2_p[:, oc:oc + 1], x2_ch(rpar, oc),
                                              OP.add, OP.add), deps=deps)
            vv_r = em.vop(lambda v, oc=oc, rpar=rpar, sq=sq:
                          v.tensor_mul(slot(sq, oc), rb_ch(rpar, oc), rb_ch(rpar, oc)))
            bank_free[bi] = (VE, vv_r)
            sfns = [lambda t_, b=b_s, oc=oc, rpar=rpar:
                    t_.matmul(pb[b][0:1, :c.T], onescol[:, :], rb_ch(rpar, oc),
                              start=(oc == 0), stop=(oc == c.NCD - 1),
                              skip_group_check=True),
                    lambda t_, b=b_q, oc=oc, sq=sq:
                    t_.matmul(pb[b][0:1, :c.T], onescol[:, :], slot(sq, oc),
                              start=(oc == 0), stop=(oc == c.NCD - 1),
                              skip_group_check=True)]
            sdeps = [(VE, vv_r)] + (stat_deps0 if oc == 0 else [])
            pending_stats.append((sdeps, sfns))
            if len(pending_stats) > 5:
                pe_group(*pending_stats.pop(0))
        while pending_stats:
            pe_stats = pe_group(*pending_stats.pop(0))
        rb_war[rpar] = pe_stats
        if kind == "wo":
            gb_pe_war[0] = pe_stats
            qb_war[0] = pe_last
        else:
            qb_war[0] = pe_stats
        rows_done = ln_rows(tag, b_s, b_q, pe_stats)
        return pe_last, pe_stats, rows_done

    hbuf_deps = [None]

    def s1(t):
        par = t % 2
        if t not in loaded_tiles:
            load_x_tile(t * c.T, par)
            loaded_tiles.add(t)
        semn = f"g_x{par}"
        xv = slab_ready[par]
        q_add_done = []
        g_done = []
        for what in ("q", "g"):
            woff = WQO if what == "q" else WGO
            for oc in range(c.NCD):
                bi = acquire(poolBC)
                fns = []
                for ic in range(c.NCD):
                    fns.append(lambda t_, bi=bi, par=par, ic=ic, oc=oc, woff=woff:
                               t_.matmul(pb[bi][:, :c.T],
                                         wsl(woff, ic, oc * 128, 128),
                                         xb_ch(par, ic), start=(ic == 0),
                                         stop=(ic == c.NCD - 1), skip_group_check=True))
                pev = pe_group([bank_free[bi], (semn, xv),
                                ("g_wq", wq_ready) if what == "q" else ("g_wg", wg_ready)],
                               fns)
                bank_free[bi] = None
                if what == "q":
                    dep = None if qb_war[0] is None else (PE, qb_war[0])
                    vvq = elu_unit(bi, slot(qb, oc), pev, c.T,
                                   bias_pos=bq_p[:, oc:oc + 1],
                                   bias_neg=bqn_p[:, oc:oc + 1],
                                   extra_dve_dep=dep)
                    q_add_done.append(vvq)
                else:
                    deps = [(PE, pev), ("s_bias", bias_done)]
                    if oc == 0:
                        if gb_ve_war[0] is not None:
                            deps.append((VE, gb_ve_war[0]))
                        if gb_pe_war[0] is not None:
                            deps.append((PE, gb_pe_war[0]))
                    av = em.sop(lambda e, bi=bi, oc=oc:
                                e.activation(slot(gb, oc), pb[bi][:, :c.T], AF.Sigmoid,
                                             bias=bg_p[:, oc:oc + 1], scale=1.0),
                                deps=deps)
                    bank_free[bi] = (ACT, av)
                    g_done.append(av)
        qb_war[0] = None
        ga_done = []
        for j in range(c.NP):
            bi = acquire(poolBC)
            fns = []
            for r in range(2):
                fns.append(lambda t_, bi=bi, j=j, r=r:
                           t_.matmul(pb[bi][r * 64:(r + 1) * 64, :c.T],
                                     kvsb[r * 64:(r + 1) * 64, j * HD:(j + 1) * HD],
                                     qb[r * 64:(r + 1) * 64, j * c.T:(j + 1) * c.T],
                                     start=True, stop=True, skip_group_check=True,
                                     tile_position=(r * 64, r * 64)))
            pev = pe_group([bank_free[bi], (VE, q_add_done[j]), kv_ev_dep], fns)
            bank_free[bi] = None
            vv2 = em.vop(lambda v, bi=bi, j=j:
                         v.tensor_mul(slot(qb, j), pb[bi][:, :c.T], slot(gb, j)),
                         deps=[(PE, pev), (ACT, g_done[j])])
            bank_free[bi] = (VE, vv2)
            ga_done.append(vv2)
        gb_ve_war[0] = ga_done[-1]
        em.waits("tensor", [(VE, ga_done[-1])])
        pe_last, pe_stats, rows_done = proj_stage(t, par, par, "wo")
        slab_war[par] = (VE, em.val(VE))
        return rows_done

    def h_stage(t):
        h_act_last = None
        if t == 0:
            em.wait("sync", "s_cv0", conv_store_final[0])
            em.wait("sync", "s_cv1", conv_store_final[1])
        pev = None
        for g_i in range(c.NOG):
            parw = g_i % 3
            em.wait("sync", PE, w1p_war[parw])
            semw = f"s_w1{parw}"
            lv = em.reserve(semw, 16)
            em.emit("sync", lambda s, g_i=g_i, parw=parw, sn=semw:
                    s.dma_start(out=wbuf[:, W1SL[parw]:W1SL[parw] + 2 * c.NCD * 128],
                                in_=w1s[g_i * 128:(g_i + 1) * 128, :])
                    .then_inc(em.handle(sn), 16))
            for o2 in range(2):
                oc = 2 * g_i + o2
                bi = acquire(poolBC)
                deps = [bank_free[bi], (semw, lv)]
                if oc == 0:
                    deps.append((VE, norm1_done[0]))
                fns = []
                for ic in range(c.NCD):
                    fns.append(lambda t_, bi=bi, ic=ic, o2=o2, parw=parw, tp=t % 2:
                               t_.matmul(pb[bi][:, :c.T],
                                         wbuf[:, W1SL[parw] + (ic * 2 + o2) * 128:
                                              W1SL[parw] + (ic * 2 + o2) * 128 + 128],
                                         x2_ch(tp, ic), start=(ic == 0),
                                         stop=(ic == c.NCD - 1), skip_group_check=True))
                pev = pe_group(deps, fns)
                bank_free[bi] = None
                sdeps = [(PE, pev), ("s_bias", bias_done)]
                if hbuf_pe_war[0] is not None:
                    sdeps.append((PE, hbuf_pe_war[0]))
                    hbuf_pe_war[0] = None
                av = em.sop(lambda e, bi=bi, oc=oc:
                            e.activation(slot(hbuf, oc), pb[bi][:, :c.T], AF.Relu,
                                         bias=b1_p[:, oc:oc + 1], scale=1.0),
                            deps=sdeps)
                bank_free[bi] = (ACT, av)
                h_act_last = av
            w1p_war[parw] = pev
        x2b_pe_war[t % 2] = pev
        hbuf_deps[0] = (ACT, h_act_last)

    norm1_done = [None]
    w2_gate_done = [False]
    kvstage_dep_box = [kvstage_dep]
    rows2_pending = None

    for t in range(c.NT):
        def write_x2(ic, tp=t % 2):
            return em.vop(lambda v, ic=ic, tp=tp:
                          v.tensor_sub(x2_ch(tp, ic), tmpN[:, :c.T], m2B[:, :c.T]))

        rows1_done = s1(t)

        def emit_ln1(t=t, rows1_done=rows1_done, write_x2=write_x2):
            fd = []
            if x2b_pe_war[t % 2] is not None:
                fd.append((PE, x2b_pe_war[t % 2]))
                x2b_pe_war[t % 2] = None
            norm1_done[0] = ln_tail(0, rows1_done, t % 2, write_x2, first_deps=fd)

        if t >= 1:
            y_last, y_stats, y_rows = proj_stage(t - 1, (t - 1) % 2, (t - 1) % 2, "y",
                                                 after_first=emit_ln1)
            hbuf_pe_war[0] = y_last
            rows2_pending = (t - 1, y_rows, y_last)
        else:
            emit_ln1()
        h_stage(t)
        if rows2_pending is not None and t >= 1:
            tp, r2_rows, _ = rows2_pending
            rows2_pending = None
            fd2 = []
            if kvstage_dep_box[0] is not None:
                fd2.append(kvstage_dep_box[0])
                kvstage_dep_box[0] = None
            if last_store[0] is not None:
                fd2.append(("s_out", last_store[0]))
                last_store[0] = None

            def write_x3(ic):
                return em.vop(lambda v, ic=ic:
                              v.tensor_sub(slot(rbuf, ic), tmpN[:, :c.T], m2B[:, :c.T]))
            nv2 = ln_tail(1, r2_rows, tp % 2, write_x3, first_deps=fd2)
            em.wait("sync", VE, nv2)
            sv = em.reserve("s_out", 16)
            t0s = tp * c.T
            em.emit("sync", lambda s, t0=t0s:
                    s.dma_start(out=out_d[:, t0 * c.NCD:(t0 + c.T) * c.NCD],
                                in_=rbuf[:, :c.NCD * c.T])
                    .then_inc(em.handle("s_out"), 16))
            last_store[0] = sv

    t = c.NT
    y_last, y_stats, y_rows = proj_stage(t - 1, (t - 1) % 2, (t - 1) % 2, "y")
    fd2 = []
    if kvstage_dep_box[0] is not None:
        fd2.append(kvstage_dep_box[0])
        kvstage_dep_box[0] = None
    if last_store[0] is not None:
        fd2.append(("s_out", last_store[0]))
        last_store[0] = None

    def write_x3f(ic):
        return em.vop(lambda v, ic=ic:
                      v.tensor_sub(slot(rbuf, ic), tmpN[:, :c.T], m2B[:, :c.T]))
    nv2 = ln_tail(1, y_rows, (t - 1) % 2, write_x3f, first_deps=fd2)
    em.wait("sync", VE, nv2)
    sv = em.reserve("s_out", 16)
    t0 = (t - 1) * c.T
    em.emit("sync", lambda s, t0=t0:
            s.dma_start(out=out_d[:, t0 * c.NCD:(t0 + c.T) * c.NCD],
                        in_=rbuf[:, :c.NCD * c.T])
            .then_inc(em.handle("s_out"), 16))
    last_store[0] = sv

    names = [n for n in em.sems if (n.startswith("s_") or n.startswith("g_")) and n != "s_out"]
    for n in names:
        if em.val(n) > 0:
            em.wait("sync", n, em.val(n))
    em.wait("sync", "s_out", em.val("s_out"))

    nc._dbg = dict(kvsb=kvsb, x2b=x2b, qb=qb, gb=gb, rbuf=rbuf, hbuf=hbuf,
                   rbA=rbA, bvB=bvB, invB=invB, m2B=m2B, orow=orow,
                   bq_p=bq_p, bkrow=bkrow, wbuf=wbuf, xb=xb)
    with nc.Block() as block:
        em.run(block)
    return nc


def kernel(L0_attn_x, L0_attn_Wq, L0_attn_bq, L0_attn_Wk, L0_attn_bk, L0_attn_Wv,
           L0_attn_bv, L0_attn_Wg, L0_attn_bg, L0_attn_Wo, L0_attn_bo,
           L0_ffn_W1, L0_ffn_b1, L0_ffn_W2, L0_ffn_b2):
    cfg = Cfg()
    x = np.asarray(L0_attn_x, dtype=np.float32)
    B, S, D = x.shape
    nc = build_nc(cfg)
    weights = {
        "wq": pack_wres(np.asarray(L0_attn_Wq, dtype=np.float32), cfg),
        "wk": pack_wres(np.asarray(L0_attn_Wk, dtype=np.float32), cfg),
        "wv": pack_wres(np.asarray(L0_attn_Wv, dtype=np.float32), cfg),
        "wg": pack_wres(np.asarray(L0_attn_Wg, dtype=np.float32), cfg),
        "wo": pack_wres(np.asarray(L0_attn_Wo, dtype=np.float32), cfg),
        "w1": pack_w1(np.asarray(L0_ffn_W1, dtype=np.float32), cfg),
        "w2": pack_w2(np.asarray(L0_ffn_W2, dtype=np.float32), cfg),
        "bq": np.ascontiguousarray(L0_attn_bq, dtype=np.float32).reshape(1, -1),
        "bk": np.ascontiguousarray(L0_attn_bk, dtype=np.float32).reshape(1, -1),
        "bv": np.ascontiguousarray(L0_attn_bv, dtype=np.float32).reshape(1, -1),
        "bg": np.ascontiguousarray(L0_attn_bg, dtype=np.float32).reshape(1, -1),
        "bo": np.ascontiguousarray(L0_attn_bo, dtype=np.float32).reshape(1, -1),
        "b1": np.ascontiguousarray(L0_ffn_b1, dtype=np.float32).reshape(1, -1),
        "b2": np.ascontiguousarray(L0_ffn_b2, dtype=np.float32).reshape(1, -1),
    }
    weights["bp"] = pack_bias(weights, cfg)
    half = cfg.OWN
    in_maps = []
    for core in range(8):
        b, hf = core // 2, core % 2
        m = dict(weights)
        m["xf"] = pack_x(np.ascontiguousarray(x[b, hf * half:(hf + 1) * half, :].T),
                         cfg)
        in_maps.append(m)
    res = run_bass_kernel_spmd(nc, in_maps, core_ids=list(range(8)))
    out = np.empty((B, S, D), dtype=np.float32)
    for core in range(8):
        b, hf = core // 2, core % 2
        out[b, hf * half:(hf + 1) * half, :] = unpack_out(res.results[core]["out"],
                                                          cfg).T
    return out
